# version 03 (final)
# baseline (speedup 1.0000x reference)
import numpy as np
import ml_dtypes

import concourse.bass as bass
import concourse.mybir as mybir
from concourse.bass_utils import run_bass_kernel_spmd

F32 = mybir.dt.float32
F32R = mybir.dt.float32r
BF16 = mybir.dt.bfloat16
AF = mybir.ActivationFunctionType
ALU = mybir.AluOpType
AX = mybir.AxisListType

D = 1024
T = 4096
NCORES = 8
KT = D // 128


class Sem:
    def __init__(self, handle):
        self.h = handle
        self.count = 0


class _Inst:
    __slots__ = ("real",)

    def __init__(self, real):
        self.real = real

    def inc(self, sem, n=1):
        sem.count += n
        if self.real is not None:
            self.real.then_inc(sem.h, n)
        return self


class Eng:

    def __init__(self, ctx, name):
        self._ctx = ctx
        self._name = name

    def wait(self, sem, v=None):
        if v is None:
            v = sem.count
        if self._ctx.active == self._name:
            self._ctx.real_engine.wait_ge(sem.h, v)

    def sem_inc(self, sem, n=1):
        sem.count += n
        if self._ctx.active == self._name:
            self._ctx.real_engine.sem_inc(sem.h, n)

    def __getattr__(self, name):
        if name.startswith("_"):
            raise AttributeError(name)
        ctx = self._ctx
        mine = self._name

        def call(*a, **k):
            if ctx.active == mine:
                return _Inst(getattr(ctx.real_engine, name)(*a, **k))
            return _Inst(None)

        return call


class Ctx:
    def __init__(self, nc):
        self.nc = nc
        self.active = None
        self.real_engine = None
        self.sems = []
        self._semctx = []
        self.te = Eng(self, "tensor")
        self.ve = Eng(self, "vector")
        self.se = Eng(self, "scalar")
        self.ge = Eng(self, "gpsimd")
        self.qe = Eng(self, "sync")

    def sem(self, name):
        cm = self.nc.semaphore(name)
        h = cm.__enter__()
        self._semctx.append(cm)
        s = Sem(h)
        self.sems.append(s)
        return s

    def close_sems(self):
        for cm in reversed(self._semctx):
            cm.__exit__(None, None, None)

    def run_passes(self, block, emit_fn):
        ctx = self

        def make(name):
            def body(real):
                ctx.active = name
                ctx.real_engine = real
                for s in ctx.sems:
                    s.count = 0
                emit_fn(ctx)
            return body

        block.sync(make("sync"))
        block.tensor(make("tensor"))
        block.vector(make("vector"))
        block.scalar(make("scalar"))
        block.gpsimd(make("gpsimd"))


class Arena:

    def __init__(self, handle, f32_cols):
        self.handle = handle
        self.off = 0
        self.cols = f32_cols

    def f32(self, cols):
        assert self.off + cols <= self.cols, f"arena overflow {self.off}+{cols}>{self.cols}"
        ap = self.handle[:, self.off:self.off + cols]
        self.off += cols
        return ap

    def bf16(self, bf16_cols):
        assert bf16_cols % 2 == 0
        return self.f32(bf16_cols // 2).bitcast(BF16)

    def view_f32(self, off, cols):
        return self.handle[:, off:off + cols]


def emit_ln(ctx, Tc, xc, yc, C, P, S, gcol=None, bcol=None):
    s_mm, s_ev = S["sems"]["mm"], S["sems"]["ev"]
    cent = S["sq8"]
    rows = S["rows"]
    meanr = rows[0:1, 0:512]
    rstd = rows[0:1, 1024:1536]
    tA = rows[:, 512:1024]
    tB = rows[:, 1536:2048]
    for n in range(Tc // 512):
        def xk(k):
            return xc[:, k * Tc + n * 512: k * Tc + n * 512 + 512]

        def yk(k):
            if callable(yc):
                return yc(k, n)
            return yc[:, k * Tc + n * 512: k * Tc + n * 512 + 512]

        def ck(k):
            return cent[:, k * 512:(k + 1) * 512]

        for k in range(KT):
            i = ctx.te.matmul(P["sum"][:, :], C["ones_col"][:, :], xk(k),
                              start=(k == 0), stop=(k == KT - 1))
        i.inc(s_mm)
        ctx.se.wait(s_mm, s_mm.count)
        ctx.se.mul(out=meanr, in_=P["sum"][:, :], mul=np.float32(1.0 / D)).inc(s_ev)
        ctx.te.wait(s_ev, s_ev.count)
        ctx.te.matmul(P["rstdb"][:, :], C["ones_row"][:, :], meanr,
                      start=True, stop=True).inc(s_mm)
        ctx.ve.wait(s_mm, s_mm.count)
        for k in range(KT):
            i = ctx.ve.tensor_sub(out=ck(k), in0=xk(k), in1=P["rstdb"][:, :])
        i.inc(s_ev)
        ctx.se.wait(s_ev, s_ev.count)
        sq_mm = {}
        for k in range(KT):
            tk = tA if (k % 2 == 0) else tB
            if k >= 2:
                ctx.se.wait(s_mm, sq_mm[k - 2])
            ctx.se.activation(out=tk, in_=ck(k), func=AF.Square).inc(s_ev)
            ctx.te.wait(s_ev, s_ev.count)
            ctx.te.matmul(P["sumsq"][:, :], C["ones_col"][:, :], tk,
                          start=(k == 0), stop=(k == KT - 1)).inc(s_mm)
            sq_mm[k] = s_mm.count
        ctx.se.wait(s_mm, s_mm.count)
        ctx.se.mul(out=rstd, in_=P["sumsq"][:, :], mul=np.float32(1.0 / D))
        ctx.se.activation(out=rstd, in_=rstd, func=AF.Ln, bias=np.float32(1e-5))
        ctx.se.activation(out=rstd, in_=rstd, func=AF.Exp, scale=np.float32(-0.5)).inc(s_ev)
        ctx.te.wait(s_ev, s_ev.count)
        ctx.te.matmul(P["rstdb"][:, :], C["ones_row"][:, :], rstd,
                      start=True, stop=True).inc(s_mm)
        ctx.ve.wait(s_mm, s_mm.count)
        for k in range(KT):
            if gcol is None:
                i = ctx.ve.tensor_mul(out=yk(k), in0=ck(k), in1=P["rstdb"][:, :])
            else:
                ctx.ve.tensor_mul(out=S["t1"], in0=ck(k), in1=P["rstdb"][:, :])
                i = ctx.ve.scalar_tensor_tensor(
                    out=yk(k), in0=S["t1"], scalar=gcol[:, k:k + 1],
                    in1=bcol[:, k:k + 1].broadcast_to([128, 512]),
                    op0=ALU.mult, op1=ALU.add)
        i.inc(s_ev)
        ctx.te.wait(s_ev, s_ev.count)
        ctx.se.wait(s_ev, s_ev.count)


def emit_dense(ctx, *, lhsT_dram, n_din, n_dout, rhs, Tc, wf, wb, psum, evac, sems):
    KTin = n_din // 128
    MT = n_dout // 128
    nsub = Tc // 512
    s_w, s_cast, s_mm, s_ev = sems["w"], sems["cast"], sems["mm"], sems["ev"]
    lt = lhsT_dram.rearrange("(k p) m -> p k m", p=128)
    ngrp = len(psum) // nsub
    ev_base = s_ev.count
    mm_after = {}
    ctx.te.wait(s_ev, s_ev.count)
    for m in range(MT):
        buf = m % 2
        ctx.qe.wait(s_cast, s_cast.count - 1 if m >= 2 else s_cast.count)
        ctx.qe.dma_start(out=wf[buf].rearrange("p (k m) -> p k m", k=KTin),
                         in_=lt[:, :, m * 128:(m + 1) * 128]).inc(s_w, 16)
        ctx.ve.wait(s_w, s_w.count)
        if m >= 2:
            ctx.ve.wait(s_mm, mm_after[m - 2])
        ctx.ve.tensor_copy(out=wb[buf], in_=wf[buf]).inc(s_cast)
        ctx.te.wait(s_cast, s_cast.count)
        grp = m % ngrp
        for n in range(nsub):
            pt = psum[grp * nsub + n]
            if m >= ngrp:
                ctx.te.wait(s_ev, ev_base + (m - ngrp) * nsub + n + 1)
            for k in range(KTin):
                i = ctx.te.matmul(pt[:, :],
                                  wb[buf][:, k * 128:(k + 1) * 128],
                                  rhs(k)[:, n * 512:(n + 1) * 512],
                                  start=(k == 0), stop=(k == KTin - 1))
            i.inc(s_mm)
            mm_after[m] = s_mm.count
            evac(m, n, pt, s_mm.count)


def emit_l0ffn(ctx, r):
    Tc = 1024
    nchunks = T // Tc
    dram, C, P, sems, A = r["dram"], r["C"], r["psum"], r["sems"], r["bufs"]
    s_x, s_ev, s_mm, s_st = sems["x"], sems["ev"], sems["mm"], sems["st"]
    xc, yc, abuf, bbuf = A["xc"], A["yc"], A["a"], A["b"]
    xin_t = dram["x_in"].rearrange("(k p) t -> p k t", p=128)
    xout_t = dram["x_out"].rearrange("(k p) t -> p k t", p=128)

    for c in range(nchunks):
        t0 = c * Tc

        def slab(buf, m, n):
            return buf[:, m * Tc + n * 512: m * Tc + n * 512 + 512]

        ctx.qe.wait(s_ev, s_ev.count)
        ctx.qe.wait(s_st, s_st.count)
        ctx.qe.dma_start(out=xc.rearrange("p (k t) -> p k t", k=KT),
                         in_=xin_t[:, :, t0:t0 + Tc]).inc(s_x, 16)
        for e in (ctx.te, ctx.ve, ctx.se):
            e.wait(s_x, s_x.count)
        emit_ln(ctx, Tc, xc, yc, C, P["ln"], r["lnS"])

        def evac_w1(m, n, pt, mmv):
            ctx.se.wait(sems["mm"], mmv)
            ctx.se.activation(out=slab(abuf, m, n), in_=pt[:, :], func=AF.Silu,
                              scale=C["s1"][:, m:m + 1]).inc(s_ev)

        def evac_w2(m, n, pt, mmv):
            ctx.se.wait(sems["mm"], mmv)
            ctx.se.mul(out=slab(bbuf, m, n), in_=pt[:, :],
                       mul=C["s2"][:, m:m + 1]).inc(s_ev)

        emit_dense(ctx, lhsT_dram=dram["w1T"], n_din=D, n_dout=4096,
                   rhs=lambda k: yc[:, k * Tc:(k + 1) * Tc], Tc=Tc,
                   wf=A["wf"], wb=A["wb"], psum=P["mm"], evac=evac_w1, sems=sems)
        emit_dense(ctx, lhsT_dram=dram["w2T"], n_din=D, n_dout=4096,
                   rhs=lambda k: yc[:, k * Tc:(k + 1) * Tc], Tc=Tc,
                   wf=A["wf"], wb=A["wb"], psum=P["mm"], evac=evac_w2, sems=sems)

        ctx.ve.wait(s_ev, s_ev.count)
        for j in range(8):
            i = ctx.ve.tensor_mul(out=abuf[:, j * 4096:(j + 1) * 4096],
                                  in0=abuf[:, j * 4096:(j + 1) * 4096],
                                  in1=bbuf[:, j * 4096:(j + 1) * 4096])
        i.inc(s_ev)

        def evac_h1(m, n, pt, mmv):
            ctx.se.wait(sems["mm"], mmv)
            ctx.se.activation(out=slab(bbuf, m, n), in_=pt[:, :], func=AF.Gelu,
                              scale=C["sh1"][:, m:m + 1]).inc(s_ev)

        emit_dense(ctx, lhsT_dram=dram["h1T"], n_din=4096, n_dout=4096,
                   rhs=lambda k: abuf[:, k * Tc:(k + 1) * Tc], Tc=Tc,
                   wf=A["hf"], wb=A["hb"], psum=P["mm"], evac=evac_h1, sems=sems)

        def evac_h2(m, n, pt, mmv):
            ctx.se.wait(sems["mm"], mmv)
            ctx.se.activation(out=slab(abuf, m, n), in_=pt[:, :], func=AF.Gelu,
                              scale=C["sh2"][:, m:m + 1]).inc(s_ev)

        emit_dense(ctx, lhsT_dram=dram["h2T"], n_din=4096, n_dout=4096,
                   rhs=lambda k: bbuf[:, k * Tc:(k + 1) * Tc], Tc=Tc,
                   wf=A["hf"], wb=A["hb"], psum=P["mm"], evac=evac_h2, sems=sems)

        ctx.qe.wait(s_ev, s_ev.count)
        ctx.qe.dma_start(out=xc.rearrange("p (k t) -> p k t", k=KT),
                         in_=xin_t[:, :, t0:t0 + Tc]).inc(s_x, 16)
        ctx.ve.wait(s_x, s_x.count)

        def evac_w3(m, n, pt, mmv):
            ctx.ve.wait(sems["mm"], mmv)
            xs = xc[:, m * Tc + n * 512: m * Tc + n * 512 + 512]
            ctx.ve.scalar_tensor_tensor(out=xs, in0=pt[:, :],
                                        scalar=C["s3h"][:, m:m + 1], in1=xs,
                                        op0=ALU.mult, op1=ALU.add).inc(s_ev)
            ctx.ge.wait(s_ev, s_ev.count)
            ctx.ge.dma_start(out=xout_t[:, m, t0 + n * 512: t0 + n * 512 + 512],
                             in_=xs).inc(s_st, 16)

        emit_dense(ctx, lhsT_dram=dram["w3T"], n_din=4096, n_dout=D,
                   rhs=lambda k: abuf[:, k * Tc:(k + 1) * Tc], Tc=Tc,
                   wf=A["hf"], wb=A["hb"], psum=P["mm"], evac=evac_w3, sems=sems)
    ctx.ge.wait(s_st, s_st.count)


def setup_l0ffn_resources(nc, ctx, ar, psums, dram):
    Tc = 1024
    bufs = {}
    bufs["a"] = ar.bf16(32 * Tc)
    b_off = ar.off
    bufs["b"] = ar.bf16(32 * Tc)
    bufs["xc"] = ar.view_f32(b_off, KT * Tc)
    bufs["yc"] = ar.bf16(KT * Tc)
    hf0_off = ar.off
    hf0 = ar.f32(32 * 128)
    hf1 = ar.f32(32 * 128)
    hb0 = ar.bf16(32 * 128)
    hb1 = ar.bf16(32 * 128)
    bufs["hf"] = [hf0, hf1]
    bufs["hb"] = [hb0, hb1]
    bufs["wf"] = [hf0[:, 0:8 * 128], hf1[:, 0:8 * 128]]
    bufs["wb"] = [hb0[:, 0:8 * 128], hb1[:, 0:8 * 128]]
    sq8 = ar.view_f32(0, 8 * 512)
    _ = hf0_off
    rows = ar.f32(2048)
    t1 = ar.f32(512)
    consts_col = ar.f32(1)
    consts_sc = ar.f32(144)
    C = {
        "ones_col": consts_col,
        "ones_row": rows[0:1, 0:128],
        "s1": consts_sc[:, 0:32], "s2": consts_sc[:, 32:64],
        "sh1": consts_sc[:, 64:96], "sh2": consts_sc[:, 96:128],
        "s3": consts_sc[:, 128:136], "s3h": consts_sc[:, 136:144],
    }
    onesrow = ar.f32(128)
    C["ones_row"] = onesrow[0:1, 0:128]
    sems = {n: ctx.sem("l0_" + n) for n in
            ["x", "w", "cast", "mm", "ev", "st", "ln_mm", "ln_ev", "init"]}
    lnS = {"sq8": sq8, "rows": rows, "t1": t1, "sems": sems}
    P = {"mm": psums["mm"], "ln": psums["ln"]}
    return {"dram": dram, "C": C, "psum": P, "bufs": bufs, "sems": sems,
            "lnS": lnS, "consts_sc": consts_sc, "consts_col": consts_col,
            "onesrow": onesrow}


def emit_l0ffn_init(ctx, r, scales_dram):
    sems = r["sems"]
    ctx.qe.dma_start(out=r["consts_sc"][:, :], in_=scales_dram[:, :]).inc(sems["x"], 16)
    ctx.ve.memset(r["consts_col"][:, :], 1.0)
    ctx.ve.memset(r["onesrow"][0:1, 0:128], 1.0)
    ctx.ve.wait(sems["x"], 16)
    ctx.ve.tensor_scalar_mul(out=r["C"]["s3h"], in0=r["C"]["s3"],
                             scalar1=0.5).inc(sems["init"])
    for e in (ctx.te, ctx.se, ctx.ge, ctx.qe):
        e.wait(sems["init"])


def build_l0ffn_program():
    nc = bass.Bass("TRN2", target_bir_lowering=False)
    dram = {
        "x_in": nc.dram_tensor("x", [D, T], F32, kind="ExternalInput").ap(),
        "w1T": nc.dram_tensor("w1T", [D, 4096], F32, kind="ExternalInput").ap(),
        "w2T": nc.dram_tensor("w2T", [D, 4096], F32, kind="ExternalInput").ap(),
        "h1T": nc.dram_tensor("h1T", [4096, 4096], F32, kind="ExternalInput").ap(),
        "h2T": nc.dram_tensor("h2T", [4096, 4096], F32, kind="ExternalInput").ap(),
        "w3T": nc.dram_tensor("w3T", [4096, D], F32, kind="ExternalInput").ap(),
        "x_out": nc.dram_tensor("y", [D, T], F32, kind="ExternalOutput").ap(),
    }
    scales = nc.dram_tensor("scales", [128, 144], F32, kind="ExternalInput").ap()
    with (
        nc.sbuf_tensor("arena", [128, 52000], F32) as arena_t,
        nc.psum_tensor("p0", [128, 512], F32) as p0,
        nc.psum_tensor("p1", [128, 512], F32) as p1,
        nc.psum_tensor("p2", [128, 512], F32) as p2,
        nc.psum_tensor("p3", [128, 512], F32) as p3,
        nc.psum_tensor("pln_a", [128, 512], F32) as pln_a,
        nc.psum_tensor("pln_b", [128, 512], F32) as pln_b,
        nc.psum_tensor("prow_a", [1, 512], F32) as prow_a,
        nc.psum_tensor("prow_b", [1, 512], F32) as prow_b,
        nc.Block() as block,
    ):
        ar = Arena(arena_t, 52000)
        ctx = Ctx(nc)
        psums = {"mm": [p0, p1, p2, p3],
                 "ln": {"sum": prow_a[:, :], "sumsq": prow_b[:, :],
                        "rstdb": pln_a, "mrstdb": pln_b}}
        r = setup_l0ffn_resources(nc, ctx, ar, psums, dram)

        def emit(c):
            emit_l0ffn_init(c, r, scales)
            emit_l0ffn(c, r)

        ctx.run_passes(block, emit)
        ctx.close_sems()
    return nc


def prep_l0ffn_inputs(inputs):
    w1T = np.ascontiguousarray(np.asarray(inputs["L0_ffn_w1_w"]).T)
    w2T = np.ascontiguousarray(np.asarray(inputs["L0_ffn_w2_w"]).T)
    h1T = np.ascontiguousarray(np.asarray(inputs["L0_ffn_h1_w"]).T)
    h2T = np.ascontiguousarray(np.asarray(inputs["L0_ffn_h2_w"]).T)
    w3T = np.ascontiguousarray(np.asarray(inputs["L0_ffn_w3_w"]).T)
    sc = np.zeros((128, 144), dtype=np.float32)
    sc[:, 0:32] = np.asarray(inputs["L0_ffn_w1_s"]).reshape(32, 128).T
    sc[:, 32:64] = np.asarray(inputs["L0_ffn_w2_s"]).reshape(32, 128).T
    sc[:, 64:96] = np.asarray(inputs["L0_ffn_h1_s"]).reshape(32, 128).T
    sc[:, 96:128] = np.asarray(inputs["L0_ffn_h2_s"]).reshape(32, 128).T
    sc[:, 128:136] = np.asarray(inputs["L0_ffn_w3_s"]).reshape(8, 128).T
    return {"w1T": w1T, "w2T": w2T, "h1T": h1T, "h2T": h2T, "w3T": w3T,
            "scales": sc}


def emit_l0attn(ctx, r):
    Tc = 1024
    nchunks = T // Tc
    dram, C, P, sems, A = r["dram"], r["C"], r["psum"], r["sems"], r["bufs"]
    s_x, s_ev, s_mm, s_st = sems["x"], sems["ev"], sems["mm"], sems["st"]
    xc, yc, hb = A["xc"], A["yc"], A["hb"]
    wf, wb = A["wf"], A["wb"]
    xin_t = dram["x_in"].rearrange("(k p) t -> p k t", p=128)
    xout_t = dram["x_out"].rearrange("(k p) t -> p k t", p=128)
    pmm = P["mm"]

    ctx.qe.dma_start(out=wf.rearrange("p (k m) -> p k m", k=KT),
                     in_=dram["WT"].rearrange("(k p) m -> p k m", p=128)).inc(s_x, 16)
    ctx.ve.wait(s_x, s_x.count)
    ctx.ve.tensor_copy(out=wb, in_=wf).inc(sems["cast"])
    ctx.ve.memset(C["onesb"][:, :], 1.0).inc(sems["cast"])
    ctx.te.wait(sems["cast"], sems["cast"].count)

    def wtile(k, m):
        return wb[:, k * 1024 + m * 128: k * 1024 + m * 128 + 128]

    for c in range(nchunks):
        t0 = c * Tc
        ctx.qe.wait(s_ev, s_ev.count)
        ctx.qe.wait(s_st, s_st.count)
        ctx.qe.dma_start(out=xc.rearrange("p (k t) -> p k t", k=KT),
                         in_=xin_t[:, :, t0:t0 + Tc]).inc(s_x, 16)
        for e in (ctx.te, ctx.ve, ctx.se):
            e.wait(s_x, s_x.count)
        emit_ln(ctx, Tc, xc, yc, C, P["ln"], r["lnS"])
        ctx.te.wait(s_ev, s_ev.count)
        r["hb_ev0"] = s_ev.count
        for m in range(KT):
            for n in range(2):
                pt = pmm[(m % 2) * 2 + n]
                if m >= 2:
                    ctx.te.wait(s_ev, r["hb_ev0"] + (m - 2) * 2 + n + 1)
                for k in range(KT):
                    i = ctx.te.matmul(pt[:, :], wtile(k, m),
                                      yc[:, k * Tc + n * 512: k * Tc + n * 512 + 512],
                                      start=(k == 0), stop=(k == KT - 1))
                i.inc(s_mm)
                ctx.se.wait(s_mm, s_mm.count)
                ctx.se.add(out=hb[:, m * Tc + n * 512: m * Tc + n * 512 + 512],
                           in_=pt[:, :], add=C["bias"][:, m:m + 1]).inc(s_ev)
        for n in range(2):
            for mi in range(KT):
                pt = pmm[(mi % 2) * 2 + n]
                ctx.te.wait(s_ev, s_ev.count)
                for k in range(mi + 1):
                    lhs = C["trib"] if k == mi else C["onesb"]
                    i = ctx.te.matmul(pt[:, :], lhs[:, :],
                                      hb[:, k * Tc + n * 512: k * Tc + n * 512 + 512],
                                      start=(k == 0), stop=(k == mi))
                i.inc(s_mm)
                ctx.ve.wait(s_mm, s_mm.count)
                xs = xc[:, mi * Tc + n * 512: mi * Tc + n * 512 + 512]
                ctx.ve.tensor_add(out=xs, in0=pt[:, :], in1=xs).inc(s_ev)
                ctx.ge.wait(s_ev, s_ev.count)
                ctx.ge.dma_start(out=xout_t[:, mi, t0 + n * 512: t0 + n * 512 + 512],
                                 in_=xs).inc(s_st, 16)
    ctx.ge.wait(s_st, s_st.count)


def setup_l0attn_resources(nc, ctx, ar, psums, dram):
    Tc = 1024
    bufs = {}
    bufs["xc"] = ar.f32(KT * Tc)
    bufs["yc"] = ar.bf16(KT * Tc)
    bufs["hb"] = ar.bf16(KT * Tc)
    bufs["wf"] = ar.f32(KT * 1024)
    bufs["wb"] = ar.bf16(KT * 1024)
    sq8 = ar.f32(8 * 512)
    rows = ar.f32(2048)
    t1 = ar.f32(512)
    consts_col = ar.f32(1)
    onesrow = ar.f32(128)
    bias = ar.f32(KT)
    trib = ar.bf16(128)
    onesb = ar.bf16(128)
    C = {"ones_col": consts_col, "ones_row": onesrow[0:1, 0:128],
         "bias": bias, "trib": trib, "onesb": onesb}
    sems = {n: ctx.sem("l0a_" + n) for n in
            ["x", "w", "cast", "mm", "ev", "st", "ln_mm", "ln_ev", "init"]}
    lnS = {"sq8": sq8, "rows": rows, "t1": t1, "sems": sems}
    return {"dram": dram, "C": C, "psum": psums, "bufs": bufs, "sems": sems,
            "lnS": lnS, "consts_col": consts_col, "onesrow": onesrow,
            "hb_ev0": 0}


def emit_l0attn_init(ctx, r, bias_dram, tri_dram):
    sems = r["sems"]
    ctx.qe.dma_start(out=r["C"]["bias"][:, :], in_=bias_dram[:, :]).inc(sems["x"], 16)
    ctx.qe.dma_start(out=r["C"]["trib"][:, :], in_=tri_dram[:, :]).inc(sems["x"], 16)
    ctx.ve.memset(r["consts_col"][:, :], 1.0)
    ctx.ve.memset(r["onesrow"][0:1, 0:128], 1.0).inc(sems["init"])
    for e in (ctx.te, ctx.se, ctx.ge, ctx.qe):
        e.wait(sems["init"])
    for e in (ctx.te, ctx.se, ctx.ve):
        e.wait(sems["x"], 32)


def build_l0attn_program():
    nc = bass.Bass("TRN2", target_bir_lowering=False)
    dram = {
        "x_in": nc.dram_tensor("x", [D, T], F32, kind="ExternalInput").ap(),
        "WT": nc.dram_tensor("WT", [D, D], F32, kind="ExternalInput").ap(),
        "x_out": nc.dram_tensor("y", [D, T], F32, kind="ExternalOutput").ap(),
    }
    bias = nc.dram_tensor("bias", [128, KT], F32, kind="ExternalInput").ap()
    tri = nc.dram_tensor("tri", [128, 128], BF16, kind="ExternalInput").ap()
    with (
        nc.sbuf_tensor("arena", [128, 40000], F32) as arena_t,
        nc.psum_tensor("p0", [128, 512], F32) as p0,
        nc.psum_tensor("p1", [128, 512], F32) as p1,
        nc.psum_tensor("p2", [128, 512], F32) as p2,
        nc.psum_tensor("p3", [128, 512], F32) as p3,
        nc.psum_tensor("pln_a", [128, 512], F32) as pln_a,
        nc.psum_tensor("pln_b", [128, 512], F32) as pln_b,
        nc.psum_tensor("prow_a", [1, 512], F32) as prow_a,
        nc.psum_tensor("prow_b", [1, 512], F32) as prow_b,
        nc.Block() as block,
    ):
        ar = Arena(arena_t, 40000)
        ctx = Ctx(nc)
        psums = {"mm": [p0, p1, p2, p3],
                 "ln": {"sum": prow_a[:, :], "sumsq": prow_b[:, :],
                        "rstdb": pln_a, "mrstdb": pln_b}}
        r = setup_l0attn_resources(nc, ctx, ar, psums, dram)

        def emit(c):
            emit_l0attn_init(c, r, bias, tri)
            emit_l0attn(c, r)

        ctx.run_passes(block, emit)
        ctx.close_sems()
    return nc


def prep_l0attn_inputs(inputs):
    WT = np.ascontiguousarray(np.asarray(inputs["L0_attn_W"]).T)
    bias = np.ascontiguousarray(np.asarray(inputs["L0_attn_bias"]).reshape(KT, 128).T)
    tri = np.triu(np.ones((128, 128))).astype(ml_dtypes.bfloat16)
    return {"WT": WT, "bias": bias, "tri": tri}



def emit_ln_into(ctx, Tc, xc, yc, C, P, S, gcol=None, bcol=None):
    emit_ln(ctx, Tc, xc, yc, C, P, S, gcol=gcol, bcol=bcol)


def emit_l1ffn(ctx, r):
    dram, C, sems, A = r["dram"], r["C"], r["sems"], r["bufs"]
    P8 = r["P8"]
    s_x, s_ev, s_mm, s_st = sems["x"], sems["ev"], sems["mm"], sems["st"]
    s_w, s_cast = sems["w"], sems["cast"]
    xin_t = dram["x_in"].rearrange("(k p) t -> p k t", p=128)
    xout_t = dram["x_out"].rearrange("(k p) t -> p k t", p=128)
    zb_t = dram["zbuf"].rearrange("(k p) t -> p k t", p=128)
    lnP = {"sum": P8[6][0:1, 0:512], "sumsq": P8[7][0:1, 0:512],
           "rstdb": P8[4], "mrstdb": P8[5]}

    def barrier_all():
        for e in (ctx.te, ctx.ve, ctx.se, ctx.ge, ctx.qe):
            for s in (s_ev, s_mm, s_st, s_x, s_w, s_cast):
                e.wait(s, s.count)

    def load_z(zc, Tc, t0, shifted_half=True):
        zcv = zc.rearrange("p (k t) -> p k t", k=KT)
        ctx.qe.dma_start(out=zcv[:, 0:4, :], in_=zb_t[:, 0:4, t0:t0 + Tc]).inc(s_x, 16)
        if shifted_half:
            if t0 == 0:
                ctx.qe.dma_start(out=zcv[:, 4:8, 1:Tc], in_=zb_t[:, 4:8, 0:Tc - 1]).inc(s_x, 16)
                ctx.ve.wait(s_x, s_x.count)
                ctx.ve.memset(zcv[:, 4:8, 0:1], 0.0).inc(s_ev)
            else:
                ctx.qe.dma_start(out=zcv[:, 4:8, :], in_=zb_t[:, 4:8, t0 - 1:t0 + Tc - 1]).inc(s_x, 16)
        else:
            ctx.qe.dma_start(out=zcv[:, 4:8, :], in_=zb_t[:, 4:8, t0:t0 + Tc]).inc(s_x, 16)

    lnS = dict(r["lnS"]); lnS["sq8"] = A["sq8_s0"]
    for c in range(4):
        t0 = c * 1024
        ctx.qe.wait(s_ev, s_ev.count)
        ctx.qe.wait(s_st, s_st.count)
        ctx.qe.dma_start(out=A["s0x"].rearrange("p (k t) -> p k t", k=KT),
                         in_=xin_t[:, :, t0:t0 + 1024]).inc(s_x, 16)
        for e in (ctx.te, ctx.ve, ctx.se):
            e.wait(s_x, s_x.count)
        emit_ln(ctx, 1024, A["s0x"], A["s0z"], C, lnP, lnS)
        ctx.ge.wait(s_ev, s_ev.count)
        ctx.ge.dma_start(out=zb_t[:, :, t0:t0 + 1024],
                         in_=A["s0z"].rearrange("p (k t) -> p k t", k=KT)).inc(s_st, 16)

    dbg = r.get("dbg")
    if dbg is not None:
        barrier_all()
        ctx.ge.dma_start(out=dbg["z0"], in_=dram["zbuf"]).inc(s_st, 16)
    for l in range(2):
        barrier_all()
        qs_c, qsr_c = C["qs"][:, l:l + 1], C["qsr"][:, l:l + 1]
        ks_c, ksr_c = C["ks"][:, l:l + 1], C["ksr"][:, l:l + 1]
        for c in range(8):
            t0 = c * 512
            ctx.qe.wait(s_ev, s_ev.count)
            load_z(A["zcq"], 512, t0, shifted_half=True)
            ctx.qe.dma_start(out=A["qcs"], in_=dram["xq_cos"][:, t0:t0 + 512]).inc(s_x, 16)
            ctx.qe.dma_start(out=A["qss"], in_=dram["xq_sin"][:, t0:t0 + 512]).inc(s_x, 16)
            ctx.qe.dma_start(out=A["kcs"], in_=dram["xk_cos"][:, t0:t0 + 512]).inc(s_x, 16)
            ctx.qe.dma_start(out=A["kss"], in_=dram["xk_sin"][:, t0:t0 + 512]).inc(s_x, 16)
            ctx.ve.wait(s_x, s_x.count)
            ctx.ve.wait(s_ev, s_ev.count)
            ctx.ve.tensor_scalar(out=A["qcs"], in0=A["qcs"], scalar1=qs_c, scalar2=8.0,
                                 op0=ALU.mult, op1=ALU.mult)
            ctx.ve.tensor_scalar(out=A["qss"], in0=A["qss"], scalar1=qsr_c, scalar2=8.0,
                                 op0=ALU.mult, op1=ALU.mult)
            ctx.ve.tensor_scalar_mul(out=A["kcs"], in0=A["kcs"], scalar1=ks_c)
            ctx.ve.tensor_scalar_mul(out=A["kss"], in0=A["kss"], scalar1=ksr_c).inc(s_ev)
            ctx.ve.wait(s_x, s_x.count)
            ctx.ve.tensor_copy(out=A["zcqb"], in_=A["zcq"]).inc(s_ev)
            for e in (ctx.te, ctx.se):
                e.wait(s_x, s_x.count)
                e.wait(s_ev, s_ev.count)

            def evac_qkv(m, n, pt, mmv):
                cls, j = divmod(m, 4)
                ctx.se.wait(sems["mm"], mmv)
                if cls == 4:
                    out = A["vbuf"][:, j * T + t0: j * T + t0 + 512]
                else:
                    dst = [A["qraw"], A["qrot"], A["kraw"], A["krot"]][cls]
                    out = dst[:, j * 512:(j + 1) * 512]
                ctx.se.copy(out=out, in_=pt[:, :]).inc(s_ev)

            emit_dense(ctx, lhsT_dram=dram["wqkv_ext"][l], n_din=D, n_dout=2560,
                       rhs=lambda k: A["zcqb"][:, k * 512:(k + 1) * 512], Tc=512,
                       wf=A["wf"], wb=A["wb"], psum=P8[0:4], evac=evac_qkv, sems=sems)

            for raw, rot, cos, sin, dstf in (
                    (A["qraw"], A["qrot"], A["qcs"], A["qss"], A["qf"]),
                    (A["kraw"], A["krot"], A["kcs"], A["kss"], A["kf"])):
                ctx.se.wait(s_ev, s_ev.count)
                for j in range(4):
                    i = ctx.se.activation(out=A["sqn"][:, j * 512:(j + 1) * 512],
                                          in_=raw[:, j * 512:(j + 1) * 512], func=AF.Square)
                i.inc(s_ev)
                ctx.te.wait(s_ev, s_ev.count)
                for j in range(4):
                    i = ctx.te.matmul(P8[4][0:8, 0:512], C["hsel"][:, j * 8:(j + 1) * 8],
                                      A["sqn"][:, j * 512:(j + 1) * 512],
                                      start=(j == 0), stop=(j == 3))
                i.inc(s_mm)
                ctx.se.wait(s_mm, s_mm.count)
                ctx.se.activation(out=A["rn"][0:8, :], in_=P8[4][0:8, 0:512],
                                  func=AF.Sqrt).inc(s_ev)
                ctx.ve.wait(s_ev, s_ev.count)
                ctx.ve.tensor_scalar_max(out=A["rn"][0:8, :], in0=A["rn"][0:8, :],
                                         scalar1=1e-12)
                ctx.ve.reciprocal(out=A["rn"][0:8, :], in_=A["rn"][0:8, :]).inc(s_ev)
                for j in range(4):
                    ctx.te.wait(s_ev, s_ev.count)
                    ctx.te.matmul(P8[5][:, 0:512], C["hbc"][0:8, j * 128:(j + 1) * 128],
                                  A["rn"][0:8, :], start=True, stop=True).inc(s_mm)
                    ctx.ve.wait(s_mm, s_mm.count)
                    rsl = slice(j * 512, (j + 1) * 512)
                    ctx.ve.tensor_mul(out=A["t1"], in0=raw[:, rsl], in1=cos[:, :])
                    ctx.ve.tensor_mul(out=A["t2b"], in0=rot[:, rsl], in1=sin[:, :])
                    ctx.ve.tensor_add(out=A["t1"], in0=A["t1"], in1=A["t2b"])
                    i = ctx.ve.tensor_mul(out=dstf[:, j * T + t0: j * T + t0 + 512],
                                          in0=A["t1"], in1=P8[5][:, 0:512])
                    i.inc(s_ev)

        if dbg is not None and l == 0:
            barrier_all()
            ctx.ge.dma_start(out=dbg["qf"], in_=A["qf"][:, :]).inc(s_st, 16)
            ctx.ge.dma_start(out=dbg["kf"], in_=A["kf"][:, :]).inc(s_st, 16)
            ctx.ge.dma_start(out=dbg["vb"], in_=A["vbuf"][:, :]).inc(s_st, 16)
        barrier_all()
        nv = 0
        for tt in range(32):
            for j in range(4):
                i = ctx.te.transpose(out=P8[nv % 2][:, 0:64].bitcast(BF16),
                                     in_=A["vbuf"][:, j * T + tt * 128: j * T + tt * 128 + 128],
                                     identity=C["idb"][:, :]).inc(s_mm)
                ctx.se.wait(s_mm, s_mm.count)
                ctx.se.copy(out=A["vT"][:, tt * 512 + j * 128: tt * 512 + j * 128 + 128],
                            in_=P8[nv % 2][:, 0:64].bitcast(BF16)).inc(s_ev)
                nv += 1
                if nv >= 2:
                    ctx.te.wait(s_ev, s_ev.count - 1)

        barrier_all()
        qpart = A["qf"].rearrange("p (j t) -> p j t", j=4)
        kpart = A["kf"].rearrange("p (j t) -> p j t", j=4)
        for w in range(8):
            for h in range(8):
                jtile, jsub = h // 2, h % 2

                def qsl(i0, ln_):
                    return qpart[jsub * 64:(jsub + 1) * 64, jtile, w * 512 + i0: w * 512 + i0 + ln_]

                def ksl(j0, ln_):
                    return kpart[jsub * 64:(jsub + 1) * 64, jtile, j0: j0 + ln_]

                for b in range(4):
                    ctx.te.wait(s_ev, s_ev.count)
                    if w > 0:
                        j0 = (w - 1) * 512 + 128 * b
                        ctx.te.matmul(P8[0][:, 0:512], qsl(128 * b, 128), ksl(j0, 512),
                                      start=True, stop=False)
                        ctx.te.matmul(P8[0][:, 0:128], C["idb"][:, :], C["mTL"][:, :],
                                      start=False, stop=True)
                        ctx.te.matmul(P8[1][:, 0:128], qsl(128 * b, 128), ksl(j0 + 512, 128),
                                      start=True, stop=False)
                        i = ctx.te.matmul(P8[1][:, 0:128], C["idb"][:, :], C["mTR"][:, :],
                                          start=False, stop=True).inc(s_mm)
                        ctx.ve.wait(s_mm, s_mm.count)
                        ctx.ve.reduce_max(out=A["mc"][:, 0:1], in_=P8[0][:, 0:512], axis=AX.X)
                        ctx.ve.reduce_max(out=A["mc"][:, 1:2], in_=P8[1][:, 0:128], axis=AX.X)
                        i = ctx.ve.tensor_max(out=A["mc"][:, 2 + b:3 + b], in0=A["mc"][:, 0:1],
                                              in1=A["mc"][:, 1:2]).inc(s_ev)
                    else:
                        nw = 128 * (b + 1)
                        ctx.te.matmul(P8[0][:, 0:nw], qsl(128 * b, 128), ksl(0, nw),
                                      start=True, stop=False)
                        i = ctx.te.matmul(P8[0][:, nw - 128:nw], C["idb"][:, :], C["mTR"][:, :],
                                          start=False, stop=True).inc(s_mm)
                        ctx.ve.wait(s_mm, s_mm.count)
                        i = ctx.ve.reduce_max(out=A["mc"][:, 2 + b:3 + b], in_=P8[0][:, 0:nw],
                                              axis=AX.X).inc(s_ev)
                ctx.te.wait(s_ev, s_ev.count)
                for b in range(4):
                    i = ctx.te.matmul(P8[6][0:1, 128 * b:128 * b + 128],
                                      A["mc"][:, 2 + b:3 + b], C["idf"][:, :],
                                      start=True, stop=True)
                i.inc(s_mm)
                ctx.se.wait(s_mm, s_mm.count)
                ctx.se.mul(out=A["mrowb"][0:1, :], in_=P8[6][0:1, 0:512],
                           mul=np.float32(-1.0)).inc(s_ev)
                jts = list(range(8)) if w > 0 else list(range(4, 8))
                for jt in jts:
                    pa = P8[2 + (jt % 2)]
                    jabs = (w - 1) * 512 + jt * 128
                    ctx.te.wait(s_ev, s_ev.count)
                    ctx.te.matmul(pa[:, 0:512], ksl(jabs, 128), qsl(0, 512),
                                  start=True, stop=False)
                    ctx.te.matmul(pa[:, 0:512], C["idb"][:, :],
                                  C["maskT"][:, jt * 512:(jt + 1) * 512],
                                  start=False, stop=False)
                    i = ctx.te.matmul(pa[:, 0:512], C["ones1b"][0:1, :], A["mrowb"][0:1, :],
                                      start=False, stop=True).inc(s_mm)
                    ctx.se.wait(s_mm, s_mm.count)
                    i = ctx.se.activation(out=A["aT"][:, jt * 512:(jt + 1) * 512],
                                          in_=pa[:, 0:512], func=AF.Exp).inc(s_ev)
                ctx.te.wait(s_ev, s_ev.count)
                for ji, jt in enumerate(jts):
                    i = ctx.te.matmul(P8[6][0:1, 0:512], C["ones_colb"][:, :],
                                      A["aT"][:, jt * 512:(jt + 1) * 512],
                                      start=(ji == 0), stop=(jt == 7))
                i.inc(s_mm)
                ctx.se.wait(s_mm, s_mm.count)
                ctx.se.copy(out=A["rsum"][0:1, :], in_=P8[6][0:1, 0:512]).inc(s_ev)
                ctx.te.wait(s_ev, s_ev.count)
                for b in range(4):
                    i = ctx.te.matmul(P8[7][:, 4 + b:5 + b],
                                      A["rsum"][0:1, 128 * b:128 * b + 128],
                                      C["onef"][0:1, 0:1], start=True, stop=True)
                i.inc(s_mm)
                ctx.ve.wait(s_mm, s_mm.count)
                ctx.ve.reciprocal(out=A["rcol"][:, 0:4], in_=P8[7][:, 4:8]).inc(s_ev)
                for b in range(4):
                    ctx.te.wait(s_ev, s_ev.count)
                    for ji, jt in enumerate(jts):
                        tt = ((w - 1) * 512 + jt * 128) // 128
                        i = ctx.te.matmul(P8[5][:, 64 * b:64 * b + 64],
                                          A["aT"][:, jt * 512 + 128 * b: jt * 512 + 128 * b + 128],
                                          A["vT"][:, tt * 512 + jtile * 128 + jsub * 64:
                                                  tt * 512 + jtile * 128 + jsub * 64 + 64],
                                          start=(ji == 0), stop=(jt == 7))
                    i.inc(s_mm)
                    ctx.se.wait(s_mm, s_mm.count)
                    ctx.se.wait(s_ev, s_ev.count)
                    ott = w * 4 + b
                    ctx.se.mul(out=A["obuf"][:, ott * 512 + h * 64: ott * 512 + h * 64 + 64],
                               in_=P8[5][:, 64 * b:64 * b + 64],
                               mul=A["rcol"][:, b:b + 1]).inc(s_ev)
                ctx.ve.wait(s_ev, s_ev.count)

        if dbg is not None and l == 0:
            barrier_all()
            ctx.ge.dma_start(out=dbg["ob"], in_=A["obuf"][:, :]).inc(s_st, 16)
        barrier_all()
        nv = 0
        for tt in range(32):
            for fb in range(4):
                i = ctx.te.transpose(out=P8[nv % 2][:, 0:64].bitcast(BF16),
                                     in_=A["obuf"][:, tt * 512 + fb * 128: tt * 512 + fb * 128 + 128],
                                     identity=C["idb"][:, :]).inc(s_mm)
                ctx.se.wait(s_mm, s_mm.count)
                ctx.se.copy(out=A["oT"][:, fb * T + tt * 128: fb * T + tt * 128 + 128],
                            in_=P8[nv % 2][:, 0:64].bitcast(BF16)).inc(s_ev)
                nv += 1
                if nv >= 2:
                    ctx.te.wait(s_ev, s_ev.count - 1)
        barrier_all()
        for c in range(4):
            t0 = c * 1024
            ctx.qe.wait(s_ev, s_ev.count)
            ctx.qe.wait(s_st, s_st.count)
            ctx.qe.dma_start(out=A["zcw"].rearrange("p (k t) -> p k t", k=KT),
                             in_=zb_t[:, :, t0:t0 + 1024]).inc(s_x, 16)
            ctx.ve.wait(s_x, s_x.count)

            def evac_wo(m, n, pt, mmv):
                ctx.ve.wait(sems["mm"], mmv)
                zs = A["zcw"][:, m * 1024 + n * 512: m * 1024 + n * 512 + 512]
                ctx.ve.tensor_add(out=zs, in0=pt[:, :], in1=zs).inc(s_ev)
                ctx.ge.wait(s_ev, s_ev.count)
                ctx.ge.dma_start(out=zb_t[:, m, t0 + n * 512: t0 + n * 512 + 512],
                                 in_=zs).inc(s_st, 16)

            emit_dense(ctx, lhsT_dram=dram["wo2"][l], n_din=512, n_dout=D,
                       rhs=lambda k: A["oT"][:, k * T + t0: k * T + t0 + 1024], Tc=1024,
                       wf=A["wfo"], wb=A["wbo"], psum=P8[0:4], evac=evac_wo, sems=sems)

        if dbg is not None and l == 0:
            barrier_all()
            ctx.ge.dma_start(out=dbg["zwo"], in_=dram["zbuf"]).inc(s_st, 16)
        barrier_all()
        lnS2 = dict(r["lnS"]); lnS2["sq8"] = A["sq8_ffn"]
        for c in range(8):
            t0 = c * 512
            ctx.qe.wait(s_ev, s_ev.count)
            ctx.qe.wait(s_st, s_st.count)
            load_z(A["zcf"], 512, t0, shifted_half=True)
            for e in (ctx.te, ctx.ve, ctx.se):
                e.wait(s_x, s_x.count)
                e.wait(s_ev, s_ev.count)
            emit_ln_into(ctx, 512, A["zcf"], A["ycf"], C, lnP, lnS2,
                         gcol=C["lng"][:, l * 8:(l + 1) * 8], bcol=C["lnb"][:, l * 8:(l + 1) * 8])

            def evac_w1p(m, n, pt, mmv):
                ctx.se.wait(sems["mm"], mmv)
                if m < 22:
                    i = ctx.se.add(out=A["ab"][:, m * 512:(m + 1) * 512], in_=pt[:, :],
                                   add=C["b1p"][:, l * 44 + m: l * 44 + m + 1])
                else:
                    i = ctx.se.activation(out=A["gb"][:, (m - 22) * 512:(m - 21) * 512],
                                          in_=pt[:, :], func=AF.Gelu,
                                          bias=C["b1p"][:, l * 44 + m: l * 44 + m + 1])
                i.inc(s_ev)

            emit_dense(ctx, lhsT_dram=dram["w1p"][l], n_din=D, n_dout=5632,
                       rhs=lambda k: A["ycf"][:, k * 512:(k + 1) * 512], Tc=512,
                       wf=A["wf"], wb=A["wb"], psum=P8[0:4], evac=evac_w1p, sems=sems)
            ctx.ve.wait(s_ev, s_ev.count)
            for j in range(3):
                j0, j1 = j * 4096, min((j + 1) * 4096, 11264)
                i = ctx.ve.tensor_mul(out=A["ab"][:, j0:j1],
                                      in0=A["ab"][:, j0:j1], in1=A["gb"][:, j0:j1])
            i.inc(s_ev)
            ctx.qe.wait(s_ev, s_ev.count)
            ctx.qe.dma_start(out=A["zcf"].rearrange("p (k t) -> p k t", k=KT),
                             in_=zb_t[:, :, t0:t0 + 512]).inc(s_x, 16)
            ctx.ve.wait(s_x, s_x.count)

            def evac_w2p(m, n, pt, mmv):
                ctx.ve.wait(sems["mm"], mmv)
                zs = A["zcf"][:, m * 512:(m + 1) * 512]
                ctx.ve.scalar_tensor_tensor(out=zs, in0=pt[:, :],
                                            scalar=C["b2"][:, l * 8 + m: l * 8 + m + 1],
                                            in1=zs, op0=ALU.add, op1=ALU.add).inc(s_ev)
                ctx.ge.wait(s_ev, s_ev.count)
                ctx.ge.dma_start(out=zb_t[:, m, t0:t0 + 512], in_=zs).inc(s_st, 16)

            emit_dense(ctx, lhsT_dram=dram["w2p"][l], n_din=2816, n_dout=D,
                       rhs=lambda k: A["ab"][:, k * 512:(k + 1) * 512], Tc=512,
                       wf=A["wf2"], wb=A["wb2"], psum=P8[0:4], evac=evac_w2p, sems=sems)

    if dbg is not None:
        barrier_all()
        ctx.ge.dma_start(out=dbg["zf0"], in_=dram["zbuf"]).inc(s_st, 16)
    barrier_all()
    for c in range(4):
        t0 = c * 1024
        ctx.qe.wait(s_ev, s_ev.count)
        ctx.qe.wait(s_st, s_st.count)
        ctx.qe.dma_start(out=A["s0z"].rearrange("p (k t) -> p k t", k=KT),
                         in_=zb_t[:, :, t0:t0 + 1024]).inc(s_x, 16)
        ctx.qe.dma_start(out=A["s0x"].rearrange("p (k t) -> p k t", k=KT),
                         in_=xin_t[:, :, t0:t0 + 1024]).inc(s_x, 16)
        ctx.ve.wait(s_x, s_x.count)
        for k in range(KT):
            for n in range(2):
                zs = A["s0z"][:, k * 1024 + n * 512: k * 1024 + n * 512 + 512]
                xs = A["s0x"][:, k * 1024 + n * 512: k * 1024 + n * 512 + 512]
                ctx.ve.scalar_tensor_tensor(out=xs, in0=zs, scalar=0.5, in1=xs,
                                            op0=ALU.mult, op1=ALU.add).inc(s_ev)
                ctx.ge.wait(s_ev, s_ev.count)
                ctx.ge.dma_start(out=xout_t[:, k, t0 + n * 512: t0 + n * 512 + 512],
                                 in_=xs).inc(s_st, 16)
    ctx.ge.wait(s_st, s_st.count)


def setup_l1ffn_resources(nc, ctx, ar_handle, P8, dram):
    V = lambda off, cols: ar_handle[:, off:off + cols]
    B = lambda off, bf16_cols: ar_handle[:, off:off + bf16_cols // 2].bitcast(BF16)
    A = {}
    A["qf"] = B(0, 16384)
    A["kf"] = B(8192, 16384)
    A["vbuf"] = B(16384, 16384)
    A["obuf"] = B(16384, 16384)
    A["zcq"] = V(24576, 4096)
    A["vT"] = B(24576, 16384)
    A["qraw"] = B(28672, 2048)
    A["qrot"] = B(29696, 2048)
    A["kraw"] = B(30720, 2048)
    A["krot"] = B(31744, 2048)
    A["aT"] = B(32768, 8192)
    A["zcqb"] = B(35840, 4096)
    A["qcs"] = B(34816, 512)
    A["qss"] = B(35072, 512)
    A["kcs"] = B(35328, 512)
    A["kss"] = B(35584, 512)
    A["s0x"] = V(0, 8192)
    A["s0z"] = V(8192, 8192)
    A["sq8_s0"] = V(16384, 4096)
    A["oT"] = B(24576, 16384)
    A["zcw"] = V(32768, 8192)
    A["wfo"] = [V(40960, 512), V(41472, 512)]
    A["wbo"] = [B(41984, 512), B(42240, 512)]
    A["ab"] = B(0, 11264)
    A["gb"] = B(8192, 11264)
    A["zcf"] = V(24576, 4096)
    A["ycf"] = B(28672, 4096)
    A["wf2"] = [V(16384, 2816), V(19200, 2816)]
    A["wb2"] = [B(30720, 2816), B(32128, 2816)]
    A["sq8_ffn"] = V(35584, 4096)
    A["wf"] = [V(40960, 1024), V(41984, 1024)]
    A["wb"] = [B(43008, 1024), B(43520, 1024)]
    off = 45056
    def take(cols):
        nonlocal off
        v = V(off, cols); off += cols; return v
    rows = take(2048)
    A["t1"] = take(512)
    A["t2b"] = take(512)
    A["rn"] = take(512)[0:8, :]
    A["sqn"] = V(32768, 2048)
    A["mc"] = take(8)
    A["mrowb"] = rows[:, 512:768].bitcast(BF16)[0:1, :]
    A["rsum"] = rows[0:1, 1024:1536]
    A["rcol"] = take(4)
    C = {}
    C["ones_col"] = take(1)
    C["onef"] = take(1)
    C["ones_row"] = take(128)[0:1, :]
    C["idf"] = take(128)
    C["idb"] = take(64).bitcast(BF16)
    C["mTL"] = take(64).bitcast(BF16)
    C["mTR"] = take(64).bitcast(BF16)
    C["ones1b"] = take(64).bitcast(BF16)[0:1, :]
    C["ones_colb"] = take(1).bitcast(BF16)[:, 0:1]
    C["hsel"] = take(32)
    C["hbc"] = take(512)[0:8, :]
    C["maskT"] = take(2048).bitcast(BF16)
    C["qs"] = take(2); C["qsr"] = take(2); C["ks"] = take(2); C["ksr"] = take(2)
    C["lng"] = take(16); C["lnb"] = take(16)
    C["b1p"] = take(88); C["b2"] = take(16)
    assert off <= 52000, off
    sems = {n: ctx.sem("l1_" + n) for n in
            ["x", "w", "cast", "mm", "ev", "st", "ln_mm", "ln_ev", "init"]}
    lnS = {"sq8": A["sq8_s0"], "rows": rows, "t1": A["t1"], "sems": sems}
    return {"dram": dram, "C": C, "P8": P8, "bufs": A, "sems": sems, "lnS": lnS}


def emit_l1ffn_init(ctx, r, cdram):
    sems, C = r["sems"], r["C"]
    ctx.qe.dma_start(out=C["idf"], in_=cdram["idf"]).inc(sems["x"], 16)
    ctx.qe.dma_start(out=C["hsel"], in_=cdram["hsel"]).inc(sems["x"], 16)
    ctx.qe.dma_start(out=C["hbc"], in_=cdram["hbc"]).inc(sems["x"], 16)
    ctx.qe.dma_start(out=C["qs"], in_=cdram["qs"]).inc(sems["x"], 16)
    ctx.qe.dma_start(out=C["qsr"], in_=cdram["qsr"]).inc(sems["x"], 16)
    ctx.qe.dma_start(out=C["ks"], in_=cdram["ks"]).inc(sems["x"], 16)
    ctx.qe.dma_start(out=C["ksr"], in_=cdram["ksr"]).inc(sems["x"], 16)
    ctx.qe.dma_start(out=C["lng"], in_=cdram["lng"]).inc(sems["x"], 16)
    ctx.qe.dma_start(out=C["lnb"], in_=cdram["lnb"]).inc(sems["x"], 16)
    ctx.qe.dma_start(out=C["b1p"], in_=cdram["b1p"]).inc(sems["x"], 16)
    ctx.qe.dma_start(out=C["b2"], in_=cdram["b2"]).inc(sems["x"], 16)
    ctx.qe.dma_start(out=C["idb"], in_=cdram["idb"]).inc(sems["x"], 16)
    ctx.qe.dma_start(out=C["mTL"], in_=cdram["mTL"]).inc(sems["x"], 16)
    ctx.qe.dma_start(out=C["mTR"], in_=cdram["mTR"]).inc(sems["x"], 16)
    ctx.qe.dma_start(out=C["maskT"], in_=cdram["maskT"]).inc(sems["x"], 16)
    ctx.ve.wait(sems["x"], sems["x"].count)
    ctx.ve.memset(C["ones_col"], 1.0)
    ctx.ve.memset(C["onef"], 1.0)
    ctx.ve.memset(C["ones_row"], 1.0)
    ctx.ve.memset(C["ones1b"], 1.0)
    ctx.ve.memset(C["ones_colb"], 1.0).inc(sems["init"])
    for e in (ctx.te, ctx.se, ctx.ge, ctx.qe):
        e.wait(sems["init"])
        e.wait(sems["x"], sems["x"].count)


def build_l1ffn_program(debug=False):
    nc = bass.Bass("TRN2", target_bir_lowering=False)
    dram = {
        "x_in": nc.dram_tensor("x", [D, T], F32, kind="ExternalInput").ap(),
        "x_out": nc.dram_tensor("y", [D, T], F32, kind="ExternalOutput").ap(),
        "zbuf": nc.dram_tensor("zbuf", [D, T], F32, kind="Internal").ap(),
        "wqkv_ext": nc.dram_tensor("wqkv_ext", [2, D, 2560], F32, kind="ExternalInput").ap(),
        "wo2": nc.dram_tensor("wo2", [2, 512, D], F32, kind="ExternalInput").ap(),
        "w1p": nc.dram_tensor("w1p", [2, D, 5632], F32, kind="ExternalInput").ap(),
        "w2p": nc.dram_tensor("w2p", [2, 2816, D], F32, kind="ExternalInput").ap(),
        "xq_cos": nc.dram_tensor("xq_cos", [128, T], BF16, kind="ExternalInput").ap(),
        "xq_sin": nc.dram_tensor("xq_sin", [128, T], BF16, kind="ExternalInput").ap(),
        "xk_cos": nc.dram_tensor("xk_cos", [128, T], BF16, kind="ExternalInput").ap(),
        "xk_sin": nc.dram_tensor("xk_sin", [128, T], BF16, kind="ExternalInput").ap(),
    }
    cdram = {
        "idf": nc.dram_tensor("c_idf", [128, 128], F32, kind="ExternalInput").ap(),
        "hsel": nc.dram_tensor("c_hsel", [128, 32], F32, kind="ExternalInput").ap(),
        "hbc": nc.dram_tensor("c_hbc", [8, 512], F32, kind="ExternalInput").ap(),
        "qs": nc.dram_tensor("c_qs", [128, 2], F32, kind="ExternalInput").ap(),
        "qsr": nc.dram_tensor("c_qsr", [128, 2], F32, kind="ExternalInput").ap(),
        "ks": nc.dram_tensor("c_ks", [128, 2], F32, kind="ExternalInput").ap(),
        "ksr": nc.dram_tensor("c_ksr", [128, 2], F32, kind="ExternalInput").ap(),
        "lng": nc.dram_tensor("c_lng", [128, 16], F32, kind="ExternalInput").ap(),
        "lnb": nc.dram_tensor("c_lnb", [128, 16], F32, kind="ExternalInput").ap(),
        "b1p": nc.dram_tensor("c_b1p", [128, 88], F32, kind="ExternalInput").ap(),
        "b2": nc.dram_tensor("c_b2", [128, 16], F32, kind="ExternalInput").ap(),
        "idb": nc.dram_tensor("c_idb", [128, 128], BF16, kind="ExternalInput").ap(),
        "mTL": nc.dram_tensor("c_mTL", [128, 128], BF16, kind="ExternalInput").ap(),
        "mTR": nc.dram_tensor("c_mTR", [128, 128], BF16, kind="ExternalInput").ap(),
        "maskT": nc.dram_tensor("c_maskT", [128, 4096], BF16, kind="ExternalInput").ap(),
    }
    with (
        nc.sbuf_tensor("arena", [128, 52000], F32) as arena_t,
        nc.psum_tensor("p0", [128, 512], F32) as p0,
        nc.psum_tensor("p1", [128, 512], F32) as p1,
        nc.psum_tensor("p2", [128, 512], F32) as p2,
        nc.psum_tensor("p3", [128, 512], F32) as p3,
        nc.psum_tensor("p4", [128, 512], F32) as p4,
        nc.psum_tensor("p5", [128, 512], F32) as p5,
        nc.psum_tensor("p6", [128, 512], F32) as p6,
        nc.psum_tensor("p7", [128, 512], F32) as p7,
        nc.Block() as block,
    ):
        ctx = Ctx(nc)
        P8 = [p0, p1, p2, p3, p4, p5, p6, p7]
        r = setup_l1ffn_resources(nc, ctx, arena_t, P8, dram)
        if debug:
            r["dbg"] = {
                "z0": nc.dram_tensor("dbg_z0", [D, T], F32, kind="ExternalOutput").ap(),
                "qf": nc.dram_tensor("dbg_qf", [128, 4 * T], BF16, kind="ExternalOutput").ap(),
                "kf": nc.dram_tensor("dbg_kf", [128, 4 * T], BF16, kind="ExternalOutput").ap(),
                "vb": nc.dram_tensor("dbg_vb", [128, 4 * T], BF16, kind="ExternalOutput").ap(),
                "ob": nc.dram_tensor("dbg_ob", [128, 32 * 512], BF16, kind="ExternalOutput").ap(),
                "zwo": nc.dram_tensor("dbg_zwo", [D, T], F32, kind="ExternalOutput").ap(),
                "zf0": nc.dram_tensor("dbg_zf0", [D, T], F32, kind="ExternalOutput").ap(),
            }

        def emit(c):
            emit_l1ffn_init(c, r, cdram)
            emit_l1ffn(c, r)

        ctx.run_passes(block, emit)
        ctx.close_sems()
    return nc


def prep_l1ffn_inputs(inputs):
    DIMH = 64
    wqkv = np.asarray(inputs["L1_ffn_wqkv"])
    wo = np.asarray(inputs["L1_ffn_wo"])
    w1 = np.asarray(inputs["L1_ffn_w1"])
    b1 = np.asarray(inputs["L1_ffn_b1"])
    w2 = np.asarray(inputs["L1_ffn_w2"])
    b2 = np.asarray(inputs["L1_ffn_b2"])
    qsc = np.asarray(inputs["L1_ffn_q_scale"])
    ksc = np.asarray(inputs["L1_ffn_k_scale"])
    lng = np.asarray(inputs["L1_ffn_ln_g"])
    lnb = np.asarray(inputs["L1_ffn_ln_b"])
    perm_rot = np.concatenate([np.arange(512).reshape(8, 64)[:, list(range(32, 64)) + list(range(32))]]).reshape(-1)
    wqkv_ext = np.zeros((2, D, 2560), dtype=np.float32)
    for l in range(2):
        q = wqkv[l][:, 0:512]; k = wqkv[l][:, 512:1024]; v = wqkv[l][:, 1024:1536]
        wqkv_ext[l][:, 0:512] = q
        wqkv_ext[l][:, 512:1024] = q[:, perm_rot]
        wqkv_ext[l][:, 1024:1536] = k
        wqkv_ext[l][:, 1536:2048] = k[:, perm_rot]
        wqkv_ext[l][:, 2048:2560] = v
    w1p = np.zeros((2, D, 5632), dtype=np.float32)
    w1p[:, :, 0:2730] = w1[:, :, 0:2730]
    w1p[:, :, 2816:2816 + 2730] = w1[:, :, 2730:5460]
    b1p = np.zeros((128, 88), dtype=np.float32)
    for l in range(2):
        bb = np.zeros(5632, dtype=np.float32)
        bb[0:2730] = b1[l][0:2730]
        bb[2816:2816 + 2730] = b1[l][2730:5460]
        b1p[:, l * 44:(l + 1) * 44] = bb.reshape(44, 128).T
    w2p = np.zeros((2, 2816, D), dtype=np.float32)
    w2p[:, 0:2730, :] = w2
    b2c = np.zeros((128, 16), dtype=np.float32)
    for l in range(2):
        b2c[:, l * 8:(l + 1) * 8] = b2[l].reshape(8, 128).T
    n = T
    half = np.arange(0, DIMH, 2, dtype=np.float64)
    inv_freq = 1.0 / 10000.0 ** (half / DIMH)
    t = np.arange(n, dtype=np.float64)
    freqs = t[:, None] * inv_freq[None, :]
    freqs = np.concatenate([freqs, freqs], axis=-1)
    base = (half + 0.4 * DIMH) / (1.4 * DIMH)
    power = (t - n // 2) / (512 // 2)
    scale = base[None, :] ** power[:, None]
    scale = np.concatenate([scale, scale], axis=-1)
    cos, sin = np.cos(freqs), np.sin(freqs)
    sgn = np.concatenate([-np.ones(32), np.ones(32)])
    qcos = (cos * scale).T
    qsin = (sin * scale).T * sgn[:, None]
    kcos = (cos / scale).T
    ksin = (sin / scale).T * sgn[:, None]
    tile2 = lambda m: np.concatenate([m, m], axis=0).astype(ml_dtypes.bfloat16)
    qs_c = np.stack([np.tile(qsc[l], 2) for l in range(2)], axis=1).astype(np.float32)
    qsr_c = np.stack([np.tile(qsc[l][np.concatenate([np.arange(32, 64), np.arange(0, 32)])], 2)
                      for l in range(2)], axis=1).astype(np.float32)
    ks_c = np.stack([np.tile(ksc[l], 2) for l in range(2)], axis=1).astype(np.float32)
    ksr_c = np.stack([np.tile(ksc[l][np.concatenate([np.arange(32, 64), np.arange(0, 32)])], 2)
                      for l in range(2)], axis=1).astype(np.float32)
    lngc = np.zeros((128, 16), dtype=np.float32)
    lnbc = np.zeros((128, 16), dtype=np.float32)
    for l in range(2):
        lngc[:, l * 8:(l + 1) * 8] = lng[l].reshape(8, 128).T
        lnbc[:, l * 8:(l + 1) * 8] = lnb[l].reshape(8, 128).T
    hsel = np.zeros((128, 32), dtype=np.float32)
    for j in range(4):
        for p in range(128):
            hsel[p, j * 8 + 2 * j + (p // 64)] = 1.0
    hbc = np.zeros((8, 512), dtype=np.float32)
    for j in range(4):
        for p in range(128):
            hbc[2 * j + p // 64, j * 128 + p] = 1.0
    NEG = np.float32(-1e30)
    mTL = np.zeros((128, 128), dtype=np.float32)
    mTR = np.zeros((128, 128), dtype=np.float32)
    for p in range(128):
        mTL[p, :p] = NEG
        mTR[p, p + 1:] = NEG
    maskT = np.zeros((128, 8, 512), dtype=np.float32)
    for jt in range(8):
        for p in range(128):
            lo = 128 * jt + p - 512
            hi = 128 * jt + p
            ii = np.arange(512)
            maskT[p, jt][(ii < lo) | (ii > hi)] = NEG
    maskT = maskT.transpose(0, 1, 2).reshape(128, 8 * 512)
    return {
        "wqkv_ext": wqkv_ext, "wo2": np.ascontiguousarray(wo.astype(np.float32)),
        "w1p": w1p, "w2p": w2p,
        "xq_cos": tile2(qcos), "xq_sin": tile2(qsin),
        "xk_cos": tile2(kcos), "xk_sin": tile2(ksin),
        "c_idf": np.eye(128, dtype=np.float32),
        "c_idb": np.eye(128).astype(ml_dtypes.bfloat16),
        "c_hsel": hsel, "c_hbc": hbc,
        "c_qs": qs_c, "c_qsr": qsr_c, "c_ks": ks_c, "c_ksr": ksr_c,
        "c_lng": lngc, "c_lnb": lnbc, "c_b1p": b1p, "c_b2": b2c,
        "c_mTL": mTL.astype(ml_dtypes.bfloat16), "c_mTR": mTR.astype(ml_dtypes.bfloat16),
        "c_maskT": maskT.astype(ml_dtypes.bfloat16),
    }



def emit_scan_core(ctx, r, nsteps=T):
    dram, C, sems, A = r["dram"], r["C"], r["sems"], r["bufs"]
    P8 = r["P8"]
    s_x, s_ev, s_mm, s_st = sems["x"], sems["ev"], sems["mm"], sems["st"]
    s_vs = sems["vs"]

    def VS(inst):
        inst.inc(s_vs)
        ctx.ve.wait(s_vs, s_vs.count)
        return inst
    if r.get("kalldump") is not None:
        ctx.ge.dma_start(out=r["kalldump"], in_=A["kall"][:, :]).inc(s_st, 16)
    SB = A["stsb"]
    PB_S, PB_V, PB_T, PB_M = P8[0], P8[1], P8[2], P8[3]
    PB_T2 = P8[2]
    ctx.ve.memset(A["dwork"][:, :], 0.0).inc(s_ev)
    ctx.se.wait(s_ev, s_ev.count)
    ctx.se.copy(out=SB[:, :], in_=A["dwork"][:, :]).inc(s_ev)
    ctx.ve.memset(A["m2b"][:, :], -1e30)
    ctx.ve.memset(A["Mst"][:, :], 1.0)
    ctx.ve.memset(A["wS"][:, :], -0.9)
    ctx.ve.tensor_copy(out=A["Mscat"], in_=C["mscat0"]).inc(s_ev)
    for e in (ctx.te, ctx.se, ctx.ge, ctx.qe):
        e.wait(s_ev, s_ev.count)

    LK, LV = 256, 16
    kz, vz, qz, oh = A["kmatZ"], A["vpadZ"], A["qbrd"], A["ohist"]
    qt_d, vt_d = dram["qt"], dram["vt"]

    def build_kmatZ(buf, c, with_zeros=False):
        for h in range(16):
            ht, sub = h // 2, h % 2
            src = A["kall"][sub * 64:(sub + 1) * 64, ht * T + c * LK: ht * T + c * LK + LK]
            dst = kz[buf][sub * 64:(sub + 1) * 64, h:16 * LK:16]
            i = ctx.ve.tensor_copy(out=dst, in_=src)
            if with_zeros:
                osub = 1 - sub
                dstz = kz[buf][osub * 64:(osub + 1) * 64, h:16 * LK:16]
                i = ctx.ve.tensor_copy(out=dstz, in_=A["zrow"][0:64, :])
        return i

    def build_vq(buf, c):
        t0 = c * LV
        for h in range(16):
            ht = h // 2
            src = vt_d[t0:t0 + LV, h * 64:(h + 1) * 64]
            dst = vz[buf][h:h + 1, :].rearrange("o (t x) -> o t x", t=LV)[:, :, 64 * ht:64 * ht + 64]
            ctx.qe.dma_start(out=dst, in_=src[None, :, :],
                             allow_slow_non_contiguous=True).inc(s_x, 16)
        for sub in range(2):
            for ht in range(8):
                h = 2 * ht + sub
                src = qt_d[t0:t0 + LV, h * 64:(h + 1) * 64]
                src_b = src[None, :, :].broadcast_to([64, LV, 64])
                dst = qz[buf][sub * 64:(sub + 1) * 64, :].rearrange(
                    "p (t x) -> p t x", t=LV)[:, :, 64 * ht:64 * ht + 64]
                ctx.ge.dma_start(out=dst, in_=src_b,
                                 allow_slow_non_contiguous=True).inc(s_x, 16)

    ctx.ve.memset(A["zrow"][:, :], 0.0)
    ctx.ve.wait(s_st, s_st.count)
    kbuild0 = build_kmatZ(0, 0, with_zeros=True)
    kbuild0.inc(s_ev)
    ctx.qe.wait(s_st, s_st.count)
    ctx.ge.wait(s_st, s_st.count)
    build_vq(0, 0)
    for e in (ctx.te, ctx.ve, ctx.se, ctx.ge):
        e.wait(s_x, s_x.count)
        e.wait(s_ev, s_ev.count)

    dwork, dmk, dpz, oP = A["dwork"], A["dmk"], A["dpz"], A["oP"]
    mask_after = 0
    flush_counts = [None, None]
    build_counts = [0, 0]
    kbuild_counts = [0, 0]
    for t in range(nsteps):
        tk, tv = t % LK, t % LV
        bk_, bv_ = 0, (t // LV) % 2
        kslice = kz[0][:, 16 * tk:16 * tk + 16]
        if tv == 0 and t > 0:
            for e in (ctx.te, ctx.ve, ctx.se, ctx.ge):
                e.wait(s_x, build_counts[bv_])
            if flush_counts[bv_] is not None:
                ctx.ve.wait(s_st, flush_counts[bv_])
            if tk == 0:
                ctx.te.wait(s_ev, kbuild_counts[0])
        ctx.te.wait(s_ev, s_ev.count)
        ctx.te.matmul(PB_V[0:16, :], kslice, SB[:, :],
                      start=True, stop=True).inc(s_mm)
        ctx.ve.wait(s_mm, s_mm.count)
        ctx.ve.wait(s_ev, mask_after)
        VS(ctx.ve.scalar_tensor_tensor(out=dwork[0:16, :], in0=PB_V[0:16, :],
                                       scalar=A["Mst"][0:16, 0:1],
                                       in1=vz[bv_][0:16, 512 * tv:512 * tv + 512],
                                       op0=ALU.mult, op1=ALU.subtract))
        VS(ctx.ve.tensor_scalar(out=dwork[0:16, :], in0=dwork[0:16, :], scalar1=-1.0,
                                scalar2=1.0, op0=ALU.max, op1=ALU.min))
        ctx.ve.sem_inc(s_ev, 1)
        ctx.ge.wait(s_ev, s_ev.count)
        ctx.ge.tensor_mul(out=dmk[0:16, :], in0=dwork[0:16, :],
                          in1=C["mask01"]).inc(s_ev)
        mask_after = s_ev.count
        ctx.se.wait(s_ev, s_ev.count)
        ctx.se.mul(out=dpz[0:16, :], in_=dmk[0:16, :], mul=A["wS"][0:16, 0:1]).inc(s_ev)
        ctx.te.matmul(PB_T[0:16, 128:256], kslice, C["idfr"][:, :],
                      start=True, stop=True).inc(s_mm)
        ctx.se.wait(s_mm, s_mm.count)
        ctx.se.copy(out=A["ktrT"], in_=PB_T[0:16, 128:256]).inc(s_ev)
        ctx.te.wait(s_ev, s_ev.count)
        ctx.te.matmul(PB_S[:, :], A["ktrT"], dpz[0:16, :],
                      start=(t == 0), stop=False).inc(s_mm)
        ctx.se.wait(s_mm, s_mm.count)
        ctx.se.copy(out=SB[:, :], in_=PB_S[:, :]).inc(s_ev)
        ref_ev = s_ev.count
        ctx.ve.wait(s_ev, ref_ev)
        VS(ctx.ve.reduce_max(out=A["m2"][:, :, None],
                             in_=SB[:, :].rearrange("p (h e) -> p h e", h=8),
                             axis=AX.X, apply_absolute_value=True))
        VS(ctx.ve.tensor_copy(out=A["m2b"][0:64, 0:16:2], in_=A["m2"][0:64, :]))
        VS(ctx.ve.tensor_copy(out=A["m2b"][64:128, 1:16:2], in_=A["m2"][64:128, :]))
        ctx.ve.sem_inc(s_ev, 1)
        ctx.te.wait(s_ev, s_ev.count)
        ctx.te.matmul(PB_T[0:16, 0:128], A["m2b"][:, :], C["idf"][:, :],
                      start=True, stop=True).inc(s_mm)
        ctx.ve.wait(s_mm, s_mm.count)
        VS(ctx.ve.reduce_max(out=A["mtrue128"][:, 0:1], in_=PB_T2[:, 0:128], axis=AX.X))
        VS(ctx.ve.tensor_mul(out=A["q1f"], in0=A["mtrue128"][:, 0:1].broadcast_to([128, 8]),
                             in1=A["Mstf"]))
        VS(ctx.ve.tensor_scalar(out=A["q1f"], in0=A["q1f"], scalar1=-10.0, scalar2=1e30,
                                op0=ALU.add, op1=ALU.mult))
        VS(ctx.ve.tensor_scalar(out=A["q1f"], in0=A["q1f"], scalar1=1e-30, scalar2=1.0,
                                op0=ALU.max, op1=ALU.min))
        VS(ctx.ve.tensor_scalar(out=A["fqf"], in0=A["q1f"], scalar1=-0.1, scalar2=1.0,
                                op0=ALU.mult, op1=ALU.add))
        VS(ctx.ve.tensor_mul(out=A["Mstf"], in0=A["Mstf"], in1=A["fqf"]))
        VS(ctx.ve.reciprocal(out=A["wSf"], in_=A["Mstf"]))
        VS(ctx.ve.tensor_scalar_mul(out=A["wSf"], in0=A["wSf"], scalar1=-0.9))
        VS(ctx.ve.tensor_mul(out=A["Mscatf"], in0=A["Mscatf"], in1=A["fqf"]))
        ctx.ve.sem_inc(s_ev, 1)
        ctx.te.wait(s_ev, s_ev.count)
        ctx.te.matmul(PB_M[:, 0:8], C["sel16"], A["Mscat"], start=True, stop=True).inc(s_mm)
        mpat_mm = s_mm.count
        ctx.ge.wait(s_ev, ref_ev)
        ctx.ge.tensor_mul(out=oP[:, :], in0=SB[:, :],
                          in1=qz[bv_][:, 512 * tv:512 * tv + 512]).inc(s_ev)
        ctx.ve.wait(s_ev, s_ev.count)
        VS(ctx.ve.reduce_sum(out=A["oc"][:, :, None],
                             in_=oP[:, :].rearrange("p (h e) -> p h e", h=8), axis=AX.X))
        ctx.ve.wait(s_mm, mpat_mm)
        VS(ctx.ve.tensor_mul(out=A["oc"], in0=A["oc"], in1=PB_M[:, 0:8]))
        VS(ctx.ve.tensor_scalar(out=oh[bv_].rearrange("p (h t) -> p h t", h=8)[:, :, tv],
                                in0=A["oc"], scalar1=-1.0, scalar2=1.0,
                                op0=ALU.max, op1=ALU.min))
        ctx.ve.sem_inc(s_ev, 1)

        td = r.get("trace_dump")
        if td is not None and t < 3:
            ctx.se.wait(s_ev, s_ev.count)
            ctx.se.copy(out=A["oP"][0:16, :], in_=PB_V[0:16, :]).inc(s_ev)
            ctx.ge.wait(s_ev, s_ev.count)
            ctx.ge.dma_start(out=td[0][t], in_=A["oP"][0:16, :]).inc(s_st, 16)
            ctx.ge.dma_start(out=td[1][t], in_=A["dwork"][0:16, :]).inc(s_st, 16)
            ctx.ge.dma_start(out=td[2][t], in_=A["dpz"][0:16, :].bitcast(F32)).inc(s_st, 16)
            ctx.se.wait(s_st, s_st.count)
            ctx.se.copy(out=A["oc2dump"][:, :], in_=A["stsb"][:, :]).inc(s_ev)
            ctx.ge.wait(s_ev, s_ev.count)
            ctx.ge.dma_start(out=td[3][t], in_=A["oc2dump"][:, :]).inc(s_st, 16)
            ctx.ge.dma_start(out=td[4][t], in_=A["mtrue"]).inc(s_st, 16)
            ctx.ge.dma_start(out=td[5][t], in_=A["Mst"][0:16, 0:1]).inc(s_st, 16)
            ctx.ge.dma_start(out=td[6][t], in_=A["wS"][0:16, 0:1]).inc(s_st, 16)
            ctx.ge.dma_start(out=td[7][t], in_=A["fq"][:, 0:1]).inc(s_st, 16)
            for e in (ctx.te, ctx.ve, ctx.se):
                e.wait(s_st, s_st.count)
        if t == 2 and r.get("sbdump") is not None:
            ctx.ge.wait(s_ev, s_ev.count)
            ctx.ge.dma_start(out=r["sbdump"], in_=A["dwork"][:, :]).inc(s_st, 16)
            ctx.ge.dma_start(out=r["sbdump2"], in_=SB[:, :].bitcast(F32)).inc(s_st, 16)
            for e in (ctx.te, ctx.ve, ctx.se):
                e.wait(s_st, s_st.count)
        if tv == LV - 1:
            mp = PB_M[:, 0:8, None].broadcast_to([128, 8, 64])
            SB3 = SB.rearrange("p (h e) -> p h e", e=64)
            ctx.ve.wait(s_ev, s_ev.count)
            VS(ctx.ve.tensor_mul(out=SB3, in0=SB3, in1=mp))
            ctx.ve.sem_inc(s_ev, 1)
            ctx.te.wait(s_ev, s_ev.count)
            ctx.te.matmul(PB_S[:, :], C["idfr"][:, :], SB[:, :],
                          start=True, stop=False).inc(s_mm)
            VS(ctx.ve.memset(A["Mst"][:, :], 1.0))
            VS(ctx.ve.memset(A["wS"][:, :], -0.9))
            VS(ctx.ve.tensor_copy(out=A["Mscat"], in_=C["mscat0"]))
            ctx.ve.sem_inc(s_ev, 1)
            ohv = oh[bv_].rearrange("p (h t) -> p h t", h=8)
            od = dram["odram"].rearrange("(h s) t -> s h t", h=8)
            ctx.ge.wait(s_ev, s_ev.count)
            ctx.ge.dma_start(out=od[:, :, t - LV + 1:t + 1], in_=ohv).inc(s_st, 16)
            flush_counts[bv_] = s_st.count
            if t + 1 < T:
                cvn = (t + 1) // LV
                ctx.qe.wait(s_ev, s_ev.count)
                build_vq(cvn % 2, cvn)
                build_counts[cvn % 2] = s_x.count
                if (t + 1) % LK == 0:
                    ctx.ve.wait(s_mm, s_mm.count)
                    build_kmatZ(0, (t + 1) // LK).inc(s_ev)
                    kbuild_counts[0] = s_ev.count



def emit_l1attn(ctx, r):
    dram, C, sems, A = r["dram"], r["C"], r["sems"], r["bufs"]
    P8 = r["P8"]
    s_x, s_ev, s_mm, s_st = sems["x"], sems["ev"], sems["mm"], sems["st"]
    s_w, s_cast = sems["w"], sems["cast"]
    xin_t = dram["x_in"].rearrange("(k p) t -> p k t", p=128)
    xout_t = dram["x_out"].rearrange("(k p) t -> p k t", p=128)
    lnP = {"sum": P8[6][0:1, 0:512], "sumsq": P8[7][0:1, 0:512],
           "rstdb": P8[4], "mrstdb": P8[5]}
    lnS = r["lnS"]
    sq8 = lnS["sq8"]

    def barrier_all():
        for e in (ctx.te, ctx.ve, ctx.se, ctx.ge, ctx.qe):
            for s in (s_ev, s_mm, s_st, s_x, s_w, s_cast):
                e.wait(s, s.count)

    yd_t = dram["ydram"].rearrange("(k p) t -> p k t", p=128)
    for c in range(4):
        t0 = c * 1024
        ctx.qe.wait(s_ev, s_ev.count)
        ctx.qe.wait(s_st, s_st.count)
        ctx.qe.dma_start(out=A["xc"].rearrange("p (k t) -> p k t", k=KT),
                         in_=xin_t[:, :, t0:t0 + 1024]).inc(s_x, 16)
        for e in (ctx.te, ctx.ve, ctx.se):
            e.wait(s_x, s_x.count)
        emit_ln(ctx, 1024, A["xc"], A["ych"], C, lnP, lnS)
        ctx.ge.wait(s_ev, s_ev.count)
        ctx.ge.dma_start(out=yd_t[:, :, t0:t0 + 1024],
                         in_=A["ych"].rearrange("p (k t) -> p k t", k=KT)).inc(s_st, 16)

    barrier_all()
    for c in range(4):
        ctx.qe.wait(s_ev, s_ev.count)
        ctx.qe.dma_start(out=A["ych"].rearrange("p (k t) -> p k t", k=KT),
                         in_=yd_t[:, :, c * 1024:(c + 1) * 1024]).inc(s_x, 16)
        for e in (ctx.te, ctx.ve, ctx.se):
            e.wait(s_x, s_x.count)

        def evac_k(m, n, pt, mmv):
            ctx.se.wait(sems["mm"], mmv)
            ctx.se.add(out=A["kall"][:, m * T + c * 1024 + n * 512: m * T + c * 1024 + n * 512 + 512],
                       in_=pt[:, :], add=C["bk"][:, m:m + 1]).inc(s_ev)

        emit_dense(ctx, lhsT_dram=dram["WkT_r"], n_din=D, n_dout=D,
                   rhs=lambda k: A["ych"][:, k * 1024:(k + 1) * 1024],
                   Tc=1024, wf=A["wf"], wb=A["wb"], psum=P8[0:4], evac=evac_k, sems=sems)

    barrier_all()
    if r.get("kalldump2") is not None:
        ctx.ge.dma_start(out=r["kalldump2"], in_=A["kall"][:, :]).inc(s_st, 16)
        barrier_all()
    sqP2 = A["sq8_p2"]
    for c in range(8):
        barrier_all()
        ctx.se.wait(s_ev, s_ev.count)
        for ht in range(8):
            i = ctx.se.activation(out=sqP2[:, ht * 512:(ht + 1) * 512],
                                  in_=A["kall"][:, ht * T + c * 512: ht * T + c * 512 + 512],
                                  func=AF.Square)
        i.inc(s_ev)
        ctx.te.wait(s_ev, s_ev.count)
        for ht in range(8):
            i = ctx.te.matmul(P8[0][0:16, 0:512], C["hselK"][:, ht * 16:(ht + 1) * 16],
                              sqP2[:, ht * 512:(ht + 1) * 512], start=(ht == 0), stop=(ht == 7))
        i.inc(s_mm)
        ctx.se.wait(s_mm, s_mm.count)
        ctx.se.activation(out=A["rn16"], in_=P8[0][0:16, 0:512], func=AF.Sqrt).inc(s_ev)
        ctx.ve.wait(s_ev, s_ev.count)
        ctx.ve.tensor_scalar_max(out=A["rn16"], in0=A["rn16"], scalar1=1e-12)
        ctx.ve.reciprocal(out=A["rn16"], in_=A["rn16"]).inc(s_ev)
        for ht in range(8):
            barrier_all()
            ctx.te.matmul(P8[1][:, 0:512], C["hbcK"][0:16, ht * 128:(ht + 1) * 128],
                          A["rn16"], start=True, stop=True).inc(s_mm)
            ctx.ve.wait(s_mm, s_mm.count)
            kv = A["kall"][:, ht * T + c * 512: ht * T + c * 512 + 512]
            ctx.ve.tensor_mul(out=kv, in0=kv, in1=P8[1][:, 0:512]).inc(s_ev)
        ctx.te.wait(s_ev, s_ev.count)

    for qv in range(2):
        barrier_all()
        wsrc = "WqT" if qv == 0 else "WvT"
        lt = dram[wsrc].rearrange("(k p) m -> p k m", p=128)
        for k in range(KT):
            buf = k % 2
            ctx.qe.wait(s_cast, s_cast.count - 1 if k >= 2 else s_cast.count)
            ctx.qe.dma_start(out=A["wstg"][buf], in_=lt[:, k, :]).inc(s_w, 16)
            ctx.ve.wait(s_w, s_w.count)
            ctx.ve.tensor_copy(out=A["wqb"][:, k * 1024:(k + 1) * 1024],
                               in_=A["wstg"][buf]).inc(s_cast)
        brow = A["bqrow"] if qv == 0 else A["bvrow"]
        bsrc = "bq_row" if qv == 0 else "bv_row"
        ctx.qe.dma_start(out=brow, in_=dram[bsrc][0:1, :]).inc(s_x, 16)
        for e in (ctx.te, ctx.ve, ctx.se):
            e.wait(s_cast, s_cast.count)
            e.wait(s_x, s_x.count)
        if r.get("wqbdump") is not None and qv == 1:
            ctx.ge.wait(s_cast, s_cast.count)
            ctx.ge.dma_start(out=r["wqbdump"], in_=A["wqb"][:, :]).inc(s_st, 16)
        ev_after = {}
        for tt in range(32):
            ybuf_ = A["ytt"][tt % 2]
            if tt >= 2:
                ctx.qe.wait(s_ev, ev_after[tt - 2])
            ctx.qe.dma_start(out=ybuf_.rearrange("p (k t) -> p k t", k=KT),
                             in_=yd_t[:, :, tt * 128:(tt + 1) * 128]).inc(s_x, 16)
            ctx.te.wait(s_x, s_x.count)
            ctx.te.wait(s_ev, s_ev.count)
            ctx.te.wait(s_st, s_st.count)
            for nb in range(2):
                pt = P8[nb]
                for k in range(KT):
                    ctx.te.matmul(pt[:, :], ybuf_[:, k * 128:(k + 1) * 128],
                                  A["wqb"][:, k * 1024 + nb * 512: k * 1024 + nb * 512 + 512],
                                  start=(k == 0), stop=False)
                i = ctx.te.matmul(pt[:, :], C["ones1b"], brow[0:1, nb * 512:(nb + 1) * 512],
                                  start=False, stop=True)
            i.inc(s_mm)
            ctx.se.wait(s_mm, s_mm.count)
            ctx.se.wait(s_st, s_st.count)
            ctx.ve.wait(s_st, s_st.count)
            if qv == 0:
                ctx.se.activation(out=A["sqn"][:, 0:512], in_=P8[0][:, :], func=AF.Square)
                i = ctx.se.activation(out=A["sqn"][:, 512:1024], in_=P8[1][:, :], func=AF.Square)
                i.inc(s_ev)
                ctx.ve.wait(s_ev, s_ev.count)
                sqv = A["sqn"][:, 0:1024].rearrange("p (h e) -> p h e", h=16)
                ctx.ve.reduce_sum(out=A["rn16t"][:, :, None], in_=sqv, axis=AX.X)
                ctx.ve.tensor_scalar_max(out=A["rn16t"], in0=A["rn16t"], scalar1=1e-24)
                i = ctx.ve.reciprocal(out=A["rn16t"], in_=A["rn16t"]).inc(s_ev)
                ctx.se.wait(s_ev, s_ev.count)
                i = ctx.se.activation(out=A["rn16t"], in_=A["rn16t"], func=AF.Sqrt).inc(s_ev)
                ctx.ve.wait(s_ev, s_ev.count)
                rb = A["rn16t"][:, :, None].broadcast_to([128, 16, 64])
                qt3 = A["qtile"].rearrange("p (h e) -> p h e", e=64)
                ctx.ve.tensor_mul(out=qt3[:, 0:8, :], in0=P8[0][:, :].rearrange("p (h e) -> p h e", e=64),
                                  in1=rb[:, 0:8, :])
                i = ctx.ve.tensor_mul(out=qt3[:, 8:16, :], in0=P8[1][:, :].rearrange("p (h e) -> p h e", e=64),
                                      in1=rb[:, 8:16, :])
                i.inc(s_ev)
            else:
                ctx.se.copy(out=A["qtile"][:, 0:512], in_=P8[0][:, :])
                i = ctx.se.copy(out=A["qtile"][:, 512:1024], in_=P8[1][:, :])
                i.inc(s_ev)
            ctx.ge.wait(s_ev, s_ev.count)
            dd = dram["qt"] if qv == 0 else dram["vt"]
            ctx.ge.dma_start(out=dd[tt * 128:(tt + 1) * 128, :], in_=A["qtile"][:, 0:1024]).inc(s_st, 16)
            ev_after[tt] = s_ev.count

    barrier_all()
    emit_scan_core(ctx, r, T)

    barrier_all()
    for c in range(4):
        t0 = c * 1024
        ctx.qe.wait(s_ev, s_ev.count)
        ctx.qe.wait(s_st, s_st.count)
        ctx.qe.dma_start(out=A["xc"].rearrange("p (k t) -> p k t", k=KT),
                         in_=xin_t[:, :, t0:t0 + 1024]).inc(s_x, 16)
        ctx.qe.dma_start(out=A["och"].rearrange("p (k t) -> p k t", k=KT),
                         in_=dram["odram"].rearrange("(k p) t -> p k t", p=128)[:, :, t0:t0 + 1024]).inc(s_x, 16)
        ctx.ve.wait(s_x, s_x.count)

        def evac_wo(m, n, pt, mmv):
            ctx.ve.wait(sems["mm"], mmv)
            xs = A["xc"][:, m * 1024 + n * 512: m * 1024 + n * 512 + 512]
            ctx.ve.scalar_tensor_tensor(out=xs, in0=pt[:, :], scalar=C["bo"][:, m:m + 1],
                                        in1=xs, op0=ALU.add, op1=ALU.add).inc(s_ev)
            ctx.ge.wait(s_ev, s_ev.count)
            ctx.ge.dma_start(out=xout_t[:, m, t0 + n * 512: t0 + n * 512 + 512],
                             in_=xs).inc(s_st, 16)

        emit_dense(ctx, lhsT_dram=dram["WoT"], n_din=D, n_dout=D,
                   rhs=lambda k: A["och"][:, k * 1024:(k + 1) * 1024], Tc=1024,
                   wf=A["wfw"], wb=A["wbw"], psum=P8[0:4], evac=evac_wo, sems=sems)
    ctx.ge.wait(s_st, s_st.count)


def setup_l1attn_resources(nc, ctx, ar_handle, P8, dram, f32r_bufs):
    V = lambda off, cols: ar_handle[:, off:off + cols]
    B = lambda off, bf16_cols: ar_handle[:, off:off + bf16_cols // 2].bitcast(BF16)
    A = {}
    A["kall"] = B(0, 32768)
    A["xc"] = V(16384, 8192)
    A["wqb"] = B(16384, 8192)
    A["wstg"] = [V(20480, 1024), V(21504, 1024)]
    A["ytt"] = [B(22528, 1024), B(23040, 1024)]
    A["qtile"] = B(23552, 2048)
    A["sq8_p2"] = V(16384, 4096)
    A["kmatZ"] = [f32r_bufs["kmatZ"], f32r_bufs["kmatZ"]]
    A["ych"] = B(24576, 8192)
    A["sq8_p0"] = V(28672, 4096)
    A["wf"] = [V(28672, 1024), V(29696, 1024)]
    A["wb"] = [B(30720, 1024), B(31232, 1024)]
    A["sqn"] = V(24576, 1024)
    A["bqrow"] = B(25600, 1024)[0:1, :]
    A["bvrow"] = B(26112, 1024)[0:1, :]
    A["rn16"] = V(26624, 512)[0:16, :]
    A["rn16t"] = V(27136, 16)
    A["qbrd"] = [B(24576, 8192), B(28672, 8192)]
    A["vpadZ"] = [B(32768, 8192), B(36864, 8192)]
    A["och"] = B(32768, 8192)
    A["wfw"] = [V(36864, 1024), V(37888, 1024)]
    A["wbw"] = [B(38912, 1024), B(39424, 1024)]
    A["stsb"] = f32r_bufs["stsb"]
    A["dwork"] = V(41472, 512)
    A["dmk"] = V(41984, 512)
    A["dpz"] = f32r_bufs["dpz"]
    A["oP"] = V(43008, 512)
    A["m2"] = V(43520, 8)
    A["m2b"] = V(43528, 16)
    A["mtrue"] = V(43544, 1)[0:16, :]
    A["oc"] = V(43552, 8)
    A["ktrT"] = f32r_bufs["ktrT"][0:16, :]
    A["ohist"] = [B(43696, 128), B(43760, 128)]
    rows = V(32768, 2048)
    t1 = V(34816, 512)
    off = 43824
    def take(cols):
        nonlocal off
        v = V(off, cols); off += cols; return v
    C = {}
    C["ones_col"] = take(1)
    C["ones_row"] = take(128)[0:1, :]
    C["idf"] = take(128)
    C["ones1b"] = take(64).bitcast(BF16)[0:1, :]
    C["sel16"] = take(128)[0:16, :]
    C["hselK"] = take(128)
    C["hbcK"] = take(1024)[0:16, :]
    C["mask01"] = take(256).bitcast(BF16)[0:16, :]
    C["mscat0"] = take(8)[0:16, :]
    C["bk"] = take(8)
    C["bo"] = take(8)
    A["zrow"] = take(256)
    A["oc2dump"] = take(512)
    A["q1f"] = take(8)
    A["fqf"] = take(8)
    A["Mstf"] = take(8)
    A["wSf"] = take(8)
    A["Mscatf"] = take(8)
    A["fq"] = A["fqf"][0:16, :]
    A["Mst"] = A["Mstf"]
    A["wS"] = A["wSf"]
    A["Mscat"] = A["Mscatf"][0:16, :]
    A["mtrue128"] = take(1)
    A["mtrue"] = A["mtrue128"][0:16, :]
    C["idfr"] = f32r_bufs["idfr"]
    assert off <= 46530, off
    sems = {n: ctx.sem("l1a_" + n) for n in ["x", "w", "cast", "mm", "ev", "st", "init", "vs"]}
    lnS = {"sq8": A["sq8_p0"], "rows": rows, "t1": t1, "sems": sems}
    return {"dram": dram, "C": C, "P8": P8, "bufs": A, "sems": sems, "lnS": lnS}


def emit_l1attn_init(ctx, r, cdram):
    sems, C = r["sems"], r["C"]
    for nm in ["idf", "sel16", "hselK", "hbcK", "mask01", "mscat0", "bk", "bo"]:
        ctx.qe.dma_start(out=C[nm], in_=cdram[nm]).inc(sems["x"], 16)
    ctx.ve.wait(sems["x"], sems["x"].count)
    ctx.ve.memset(C["ones_col"], 1.0)
    ctx.ve.memset(C["ones_row"], 1.0)
    ctx.ve.tensor_copy(out=C["idfr"], in_=C["idf"])
    ctx.ve.memset(C["ones1b"], 1.0).inc(sems["init"])
    for e in (ctx.te, ctx.se, ctx.ge, ctx.qe):
        e.wait(sems["init"])
        e.wait(sems["x"], sems["x"].count)


def build_l1attn_program(debug=False):
    nc = bass.Bass("TRN2", target_bir_lowering=False)
    dram = {
        "x_in": nc.dram_tensor("x", [D, T], F32, kind="ExternalInput").ap(),
        "x_out": nc.dram_tensor("y", [D, T], F32, kind="ExternalOutput").ap(),
        "WkT_r": nc.dram_tensor("WkT_r", [D, D], F32, kind="ExternalInput").ap(),
        "WqT": nc.dram_tensor("WqT", [D, D], F32, kind="ExternalInput").ap(),
        "WvT": nc.dram_tensor("WvT", [D, D], F32, kind="ExternalInput").ap(),
        "WoT": nc.dram_tensor("WoT", [D, D], F32, kind="ExternalInput").ap(),
        "bq_row": nc.dram_tensor("bq_row", [1, D], BF16, kind="ExternalInput").ap(),
        "bv_row": nc.dram_tensor("bv_row", [1, D], BF16, kind="ExternalInput").ap(),
        "ydram": nc.dram_tensor("ydram", [D, T], BF16, kind="Internal").ap(),
        "qt": nc.dram_tensor("qt", [T, D], BF16,
                             kind="ExternalOutput" if debug else "Internal").ap(),
        "vt": nc.dram_tensor("vt", [T, D], BF16,
                             kind="ExternalOutput" if debug else "Internal").ap(),
        "odram": nc.dram_tensor("odram", [D, T], BF16,
                                kind="ExternalOutput" if debug else "Internal").ap(),
    }
    cdram = {
        "idf": nc.dram_tensor("c_idf", [128, 128], F32, kind="ExternalInput").ap(),
        "sel16": nc.dram_tensor("c_sel16", [16, 128], F32, kind="ExternalInput").ap(),
        "hselK": nc.dram_tensor("c_hselK", [128, 128], F32, kind="ExternalInput").ap(),
        "hbcK": nc.dram_tensor("c_hbcK", [16, 1024], F32, kind="ExternalInput").ap(),
        "mask01": nc.dram_tensor("c_mask01", [16, 512], BF16, kind="ExternalInput").ap(),
        "mscat0": nc.dram_tensor("c_mscat0", [16, 8], F32, kind="ExternalInput").ap(),
        "bk": nc.dram_tensor("c_bk", [128, 8], F32, kind="ExternalInput").ap(),
        "bo": nc.dram_tensor("c_bo", [128, 8], F32, kind="ExternalInput").ap(),
    }
    with (
        nc.sbuf_tensor("arena", [128, 46530], F32) as arena_t,
        nc.sbuf_tensor("r_stsb", [128, 512], F32R) as r_stsb,
        nc.sbuf_tensor("r_dpz", [128, 512], F32R) as r_dpz,
        nc.sbuf_tensor("r_kmatZ", [128, 4096], F32R) as r_kmatZ,
        nc.sbuf_tensor("r_ktrT", [128, 128], F32R) as r_ktrT,
        nc.sbuf_tensor("r_idfr", [128, 128], F32R) as r_idfr,
        nc.psum_tensor("p0", [128, 512], F32) as p0,
        nc.psum_tensor("p1", [128, 512], F32) as p1,
        nc.psum_tensor("p2", [128, 512], F32) as p2,
        nc.psum_tensor("p3", [128, 512], F32) as p3,
        nc.psum_tensor("p4", [128, 512], F32) as p4,
        nc.psum_tensor("p5", [128, 512], F32) as p5,
        nc.psum_tensor("p6", [128, 512], F32) as p6,
        nc.psum_tensor("p7", [128, 512], F32) as p7,
        nc.Block() as block,
    ):
        ctx = Ctx(nc)
        P8 = [p0, p1, p2, p3, p4, p5, p6, p7]
        f32r_bufs = {"stsb": r_stsb[:, :], "dpz": r_dpz[:, :],
                     "kmatZ": r_kmatZ[:, :], "ktrT": r_ktrT[:, :],
                     "idfr": r_idfr[:, :]}
        r = setup_l1attn_resources(nc, ctx, arena_t, P8, dram, f32r_bufs)
        if debug:
            r["kalldump"] = nc.dram_tensor("kalldump", [128, 32768], BF16,
                                           kind="ExternalOutput").ap()
            r["wqbdump"] = nc.dram_tensor("wqbdump", [128, 8192], BF16,
                                          kind="ExternalOutput").ap()
            r["kalldump2"] = nc.dram_tensor("kalldump2", [128, 32768], BF16,
                                            kind="ExternalOutput").ap()
            r["sbdump"] = nc.dram_tensor("sbdump", [128, 512], F32, kind="ExternalOutput").ap()
            r["sbdump2"] = nc.dram_tensor("sbdump2", [128, 512], F32, kind="ExternalOutput").ap()

        def emit(c):
            emit_l1attn_init(c, r, cdram)
            emit_l1attn(c, r)

        ctx.run_passes(block, emit)
        ctx.close_sems()
    return nc


def prep_l1attn_inputs(inputs):
    Wq = np.asarray(inputs["L1_attn_Wq"])
    Wk = np.asarray(inputs["L1_attn_Wk"])
    Wv = np.asarray(inputs["L1_attn_Wv"])
    Wo = np.asarray(inputs["L1_attn_Wo"])
    bq = np.asarray(inputs["L1_attn_bq"])
    bk = np.asarray(inputs["L1_attn_bk"])
    bv = np.asarray(inputs["L1_attn_bv"])
    bo = np.asarray(inputs["L1_attn_bo"])
    perm = np.zeros(D, dtype=np.int64)
    for ht in range(8):
        for p in range(128):
            perm[ht * 128 + p] = (2 * ht + p // 64) * 64 + p % 64
    WkT_r = np.ascontiguousarray(Wk.T[:, perm])
    bk_c = np.ascontiguousarray(bk[perm].reshape(8, 128).T)
    bo_c = np.ascontiguousarray(bo.reshape(8, 128).T)
    hselK = np.zeros((128, 128), dtype=np.float32)
    for ht in range(8):
        for p in range(128):
            hselK[p, ht * 16 + 2 * ht + p // 64] = 1.0
    hbcK = np.zeros((16, 8 * 128), dtype=np.float32)
    for ht in range(8):
        for p in range(128):
            hbcK[2 * ht + p // 64, ht * 128 + p] = 1.0
    sel16 = np.zeros((16, 128), dtype=np.float32)
    for h in range(16):
        for p in range(128):
            if p // 64 == h % 2:
                sel16[h, p] = 1.0
    mask01 = np.zeros((16, 512), dtype=np.float32)
    for h in range(16):
        mask01[h, (h // 2) * 64:(h // 2) * 64 + 64] = 1.0
    mscat0 = np.zeros((16, 8), dtype=np.float32)
    for h in range(16):
        mscat0[h, h // 2] = 1.0
    return {
        "WkT_r": WkT_r, "WqT": np.ascontiguousarray(Wq.T),
        "WvT": np.ascontiguousarray(Wv.T), "WoT": np.ascontiguousarray(Wo.T),
        "bq_row": bq[None, :].astype(ml_dtypes.bfloat16),
        "bv_row": bv[None, :].astype(ml_dtypes.bfloat16),
        "c_idf": np.eye(128, dtype=np.float32), "c_sel16": sel16,
        "c_hselK": hselK, "c_hbcK": hbcK,
        "c_mask01": mask01.astype(ml_dtypes.bfloat16), "c_mscat0": mscat0,
        "c_bk": bk_c, "c_bo": bo_c,
    }


def build_scan_mini(nsteps=48):
    nc = bass.Bass("TRN2", target_bir_lowering=False)
    dram = {
        "x_in": nc.dram_tensor("x", [D, T], F32, kind="ExternalInput").ap(),
        "x_out": nc.dram_tensor("y", [D, T], F32, kind="ExternalOutput").ap(),
        "WkT_r": nc.dram_tensor("WkT_r", [D, D], F32, kind="ExternalInput").ap(),
        "WqT": nc.dram_tensor("WqT", [D, D], F32, kind="ExternalInput").ap(),
        "WvT": nc.dram_tensor("WvT", [D, D], F32, kind="ExternalInput").ap(),
        "WoT": nc.dram_tensor("WoT", [D, D], F32, kind="ExternalInput").ap(),
        "bq_row": nc.dram_tensor("bq_row", [1, D], BF16, kind="ExternalInput").ap(),
        "bv_row": nc.dram_tensor("bv_row", [1, D], BF16, kind="ExternalInput").ap(),
        "ydram": nc.dram_tensor("ydram", [D, T], BF16, kind="Internal").ap(),
        "qt": nc.dram_tensor("qt", [T, D], BF16, kind="ExternalInput").ap(),
        "vt": nc.dram_tensor("vt", [T, D], BF16, kind="ExternalInput").ap(),
        "odram": nc.dram_tensor("odram", [D, T], BF16, kind="ExternalOutput").ap(),
        "kallin": nc.dram_tensor("kallin", [128, 32768], BF16, kind="ExternalInput").ap(),
    }
    cdram = {
        "idf": nc.dram_tensor("c_idf", [128, 128], F32, kind="ExternalInput").ap(),
        "sel16": nc.dram_tensor("c_sel16", [16, 128], F32, kind="ExternalInput").ap(),
        "hselK": nc.dram_tensor("c_hselK", [128, 128], F32, kind="ExternalInput").ap(),
        "hbcK": nc.dram_tensor("c_hbcK", [16, 1024], F32, kind="ExternalInput").ap(),
        "mask01": nc.dram_tensor("c_mask01", [16, 512], BF16, kind="ExternalInput").ap(),
        "mscat0": nc.dram_tensor("c_mscat0", [16, 8], F32, kind="ExternalInput").ap(),
        "bk": nc.dram_tensor("c_bk", [128, 8], F32, kind="ExternalInput").ap(),
        "bo": nc.dram_tensor("c_bo", [128, 8], F32, kind="ExternalInput").ap(),
    }
    with (
        nc.sbuf_tensor("arena", [128, 46530], F32) as arena_t,
        nc.sbuf_tensor("r_stsb", [128, 512], F32R) as r_stsb,
        nc.sbuf_tensor("r_dpz", [128, 512], F32R) as r_dpz,
        nc.sbuf_tensor("r_kmatZ", [128, 4096], F32R) as r_kmatZ,
        nc.sbuf_tensor("r_ktrT", [128, 128], F32R) as r_ktrT,
        nc.sbuf_tensor("r_idfr", [128, 128], F32R) as r_idfr,
        nc.psum_tensor("p0", [128, 512], F32) as p0,
        nc.psum_tensor("p1", [128, 512], F32) as p1,
        nc.psum_tensor("p2", [128, 512], F32) as p2,
        nc.psum_tensor("p3", [128, 512], F32) as p3,
        nc.psum_tensor("p4", [128, 512], F32) as p4,
        nc.psum_tensor("p5", [128, 512], F32) as p5,
        nc.psum_tensor("p6", [128, 512], F32) as p6,
        nc.psum_tensor("p7", [128, 512], F32) as p7,
        nc.Block() as block,
    ):
        ctx = Ctx(nc)
        P8 = [p0, p1, p2, p3, p4, p5, p6, p7]
        f32r_bufs = {"stsb": r_stsb[:, :], "dpz": r_dpz[:, :],
                     "kmatZ": r_kmatZ[:, :], "ktrT": r_ktrT[:, :],
                     "idfr": r_idfr[:, :]}
        r = setup_l1attn_resources(nc, ctx, arena_t, P8, dram, f32r_bufs)
        r["sbd_f32"] = nc.dram_tensor("sbd", [128, 512], F32, kind="ExternalOutput").ap()
        tdt = [nc.dram_tensor(f"trc{i}", [3, 16 if i < 3 else 128, 512], F32,
                              kind="ExternalOutput").ap() for i in range(4)]
        tdt += [nc.dram_tensor(f"trc{i}", [3, 16, 1], F32,
                               kind="ExternalOutput").ap() for i in range(4, 8)]
        r["trace_dump"] = tdt

        def emit(c):
            emit_l1attn_init(c, r, cdram)
            emit_scan_only(c, r, nsteps)

        ctx.run_passes(block, emit)
        ctx.close_sems()
    return nc


def emit_scan_only(ctx, r, nsteps):
    dram, C, sems, A = r["dram"], r["C"], r["sems"], r["bufs"]
    s_x, s_ev, s_st = sems["x"], sems["ev"], sems["st"]
    ctx.qe.dma_start(out=A["kall"][:, :], in_=dram["kallin"][:, :]).inc(s_x, 16)
    for e in (ctx.te, ctx.ve, ctx.se, ctx.ge):
        e.wait(s_x, s_x.count)
    emit_scan_core(ctx, r, nsteps)
    ctx.se.wait(sems["ev"], sems["ev"].count)
    ctx.se.copy(out=A["dwork"][:, :], in_=A["stsb"][:, :]).inc(sems["ev"])
    ctx.ge.wait(sems["ev"], sems["ev"].count)
    ctx.ge.wait(sems["st"], sems["st"].count)
    ctx.ge.dma_start(out=r["sbd_f32"], in_=A["dwork"][:, :]).inc(sems["st"], 16)
    ctx.ge.wait(sems["st"], sems["st"].count)


def kernel(**inputs):
    from concourse.bass_utils import run_bass_kernel_spmd

    x = np.asarray(inputs["L0_attn_x"], dtype=np.float32)
    cores = list(range(NCORES))

    m_l0f = prep_l0ffn_inputs(inputs)
    m_l0a = prep_l0attn_inputs(inputs)
    m_l1f = prep_l1ffn_inputs(inputs)
    m_l1a = prep_l1attn_inputs(inputs)

    nc_l0f = build_l0ffn_program()
    nc_l0a = build_l0attn_program()
    nc_l1f = build_l1ffn_program()
    nc_l1a = build_l1attn_program()

    cur = [np.ascontiguousarray(x[b].T) for b in range(NCORES)]

    def launch(nc, wmap):
        nonlocal cur
        im = [dict(wmap, x=cur[b]) for b in range(NCORES)]
        res = run_bass_kernel_spmd(nc, im, core_ids=cores)
        cur = [np.asarray(res.results[b]["y"]) for b in range(NCORES)]

    launch(nc_l0f, m_l0f)
    launch(nc_l0a, m_l0a)
    launch(nc_l0f, m_l0f)
    launch(nc_l1f, m_l1f)
    launch(nc_l1a, m_l1a)
    launch(nc_l1f, m_l1f)

    return np.stack([cur[b].T for b in range(NCORES)]).astype(np.float32)
